# version 05
# baseline (speedup 1.0000x reference)
import contextlib
from collections import defaultdict

import numpy as np
import ml_dtypes

import concourse.bass as bass
import concourse.mybir as mybir
from concourse.bass_utils import run_bass_kernel_spmd

F32 = mybir.dt.float32
BF16 = mybir.dt.bfloat16
AF = mybir.ActivationFunctionType
ALU = mybir.AluOpType
AX = mybir.AxisListType

T, D, FF, NE = 2048, 1024, 2048, 8
NT, NTH, NPAIR = 16, 8, 8
LNEPS, ZEPS = 1e-5, 1e-6

ENG_SEM = {"tensor": "sPE", "vector": "sVE", "scalar": "sAC", "gpsimd": "sPO"}


class P:
    def __init__(self, nc):
        self.nc = nc
        self.q = {k: [] for k in ("sync", "tensor", "vector", "scalar", "gpsimd")}
        self.sem = {}
        self.c = defaultdict(int)

    def mksem(self, ctx, *names):
        for n in names:
            self.sem[n] = ctx.enter_context(self.nc.semaphore(n))

    def wait(self, eng, sem, val=None):
        v = self.c[sem] if val is None else val
        if v <= 0:
            return
        h = self.sem[sem]
        self.q[eng].append(lambda E: E.wait_ge(h, v))

    def op(self, eng, fn, inc=None, n=1):
        if inc is None:
            self.q[eng].append(fn)
            return None
        self.c[inc] += n
        h = self.sem[inc]
        self.q[eng].append(lambda E: fn(E).then_inc(h, n))
        return self.c[inc]

    def gop(self, eng, fn):
        v = self.op(eng, fn, inc=ENG_SEM[eng])
        self.wait(eng, ENG_SEM[eng], v)
        return v

    def wait_marks(self, eng, marks):
        for sem, val in marks:
            self.wait(eng, sem, val)

    def dma(self, eng, out, in_, sem):
        return self.op(eng, lambda E: E.dma_start(out=out, in_=in_), inc=sem, n=16)


class CastStream:

    def __init__(self, p, slots, slot_sems, cast_sems, engines):
        self.p, self.slots, self.slot_sems, self.engines = p, slots, slot_sems, engines
        self.cast_sems = cast_sems
        self.k = 0
        self.kc = 0
        self.pending = []

    def load(self, src_ap, dst_ap):
        s = self.k % len(self.slots)
        if self.k >= len(self.slots):
            self.p.wait("sync", self.cast_sems[s], self.k // len(self.slots))
        self.p.dma("sync", self.slots[s], src_ap, self.slot_sems[s])
        self.pending.append((self.k, dst_ap, self.p.c[self.slot_sems[s]]))
        self.k += 1

    def drain(self, count=None, dst_free=None):
        p = self.p
        todo = self.pending if count is None else self.pending[:count]
        self.pending = self.pending[len(todo):]
        for idx, dst, dval in todo:
            eng = self.engines[self.kc % len(self.engines)]
            s = idx % len(self.slots)
            p.wait(eng, self.slot_sems[s], dval)
            if dst_free is not None:
                p.wait(eng, dst_free[0], dst_free[1])
            src = self.slots[s]
            if eng == "scalar":
                p.op(eng, lambda E, d=dst, sp=src: E.copy(out=d, in_=sp),
                     inc=self.cast_sems[s])
            else:
                p.op(eng, lambda E, d=dst, sp=src: E.tensor_copy(out=d, in_=sp),
                     inc=self.cast_sems[s])
            self.kc += 1
        return [(cs, p.c[cs]) for cs in self.cast_sems]


def build_program():
    nc = bass.Bass("TRN2", target_bir_lowering=False)

    def din(name, shape, dt=F32):
        return nc.dram_tensor(name, shape, dt, kind="ExternalInput").ap()

    x_in = din("x", [T, D])
    y_out = nc.dram_tensor("y", [T, D], F32, kind="ExternalOutput").ap()
    wqT, wkT, wvT, woT = (din(n, [D, D]) for n in ("wqT", "wkT", "wvT", "woT"))
    cbias = din("cbias", [4, D])
    router = din("router", [D, NE])
    w_in_e, w_sw_e = din("w_in_e", [NE, D, FF]), din("w_sw_e", [NE, D, FF])
    w_out_e = din("w_out_e", [NE, FF, D])
    w_in_s, w_sw_s = din("w_in_s", [D, FF]), din("w_sw_s", [D, FF])
    w_out_s = din("w_out_s", [FF, D])
    c_id_bf = din("c_id_bf", [128, 128], BF16)
    c_id_f = din("c_id_f", [128, 128], F32)
    c_ones_bf = din("c_ones_bf", [1, 512], BF16)
    c_selp = din("c_selp", [16, 1024], F32)
    dgT = nc.dram_tensor("dgT", [64, 128], BF16, kind="Internal").ap()

    with contextlib.ExitStack() as ctx:
        E_ = ctx.enter_context
        xh = E_(nc.sbuf_tensor("xh", [128, 8192], F32))
        lnT = E_(nc.sbuf_tensor("lnT", [128, 8192], BF16))
        qh = E_(nc.sbuf_tensor("qh", [128, 8192], BF16))
        wbfA = E_(nc.sbuf_tensor("wbfA", [128, 16384], BF16))
        wbfB = E_(nc.sbuf_tensor("wbfB", [128, 16384], BF16))
        wod = E_(nc.sbuf_tensor("wod", [128, 16384], BF16))
        landA = E_(nc.sbuf_tensor("landA", [128, 3072], F32))
        landO = E_(nc.sbuf_tensor("landO", [128, 1024], F32))
        gbuf = E_(nc.sbuf_tensor("gbuf", [128, 1024], F32))
        m2 = E_(nc.sbuf_tensor("m2", [128, 1024], F32))
        scr8 = E_(nc.sbuf_tensor("scr8", [128, 2048], F32))
        kvb = E_(nc.sbuf_tensor("kvb", [128, 1024], BF16))
        ksel = E_(nc.sbuf_tensor("ksel", [128, 128], BF16))
        biasr = E_(nc.sbuf_tensor("biasr", [65, 1024], BF16))
        onesb = E_(nc.sbuf_tensor("onesb", [65, 512], BF16))
        identb = E_(nc.sbuf_tensor("identb", [128, 128], BF16))
        routerf = E_(nc.sbuf_tensor("routerf", [128, 64], F32))
        routerb = E_(nc.sbuf_tensor("routerb", [128, 64], BF16))
        gfm = E_(nc.sbuf_tensor("gfm", [65, 3072], BF16))
        st1 = E_(nc.sbuf_tensor("st1", [128, 32], F32))
        st2 = E_(nc.sbuf_tensor("st2", [128, 16], F32))
        bn12w = E_(nc.sbuf_tensor("bn12w", [128, 96], F32))
        bn12 = bn12w[:, 0:12]
        mvv = E_(nc.sbuf_tensor("mvv", [128, 2], F32))
        scall = E_(nc.sbuf_tensor("scall", [128, 64], F32))
        mskb = E_(nc.sbuf_tensor("mskb", [128, 64], F32))
        mxb = E_(nc.sbuf_tensor("mxb", [128, 16], F32))
        gatb = E_(nc.sbuf_tensor("gatb", [128, 64], F32))
        gTb = E_(nc.sbuf_tensor("gTb", [64, 128], BF16))
        gatbb = E_(nc.sbuf_tensor("gatbb", [128, 64], BF16))
        PS = E_(nc.psum_tensor("PS", [128, 4096], F32))

        p = P(nc)
        p.mksem(ctx, "sLXa", "sLXb", "sW0", "sW1", "sW2",
                "sC0", "sC1", "sC2", "sO0", "sO1",
                "sD0", "sD1", "sLC", "sSELP", "sGF", "sST",
                "sPE", "sVE", "sAC", "sPO")

        def bank(b):
            return PS[:, 512 * b:512 * (b + 1)]

        def bank_bf(b):
            return PS[:, 512 * b:512 * (b + 1)].bitcast(BF16)

        elu_scr = [scr8[:, 0:1024], scr8[:, 1024:2048]]
        zbuf = scr8[:, 1024:2048]
        rt = scr8.bitcast(BF16)
        zr1 = m2[0:16, :]
        zr2 = scr8[0:16, 0:1024]
        selp = gbuf[0:16, :]
        gslot = [gbuf.bitcast(BF16)[:, 0:512], gbuf.bitcast(BF16)[:, 512:1024]]
        m2s = [m2[:, 0:512], m2[:, 512:1024]]
        kxs = [qh[:, 0:1024], qh[:, 1024:2048]]
        vxs = [qh[:, 2048:3088], qh[:, 3088:4128]]
        aT = qh[:, 4160:5184]
        xa32 = lnT.bitcast(F32)
        lnts = [qh[:, 5184:6208], qh[:, 6208:7232]]
        cb32 = landA[0:65, 0:1024]
        cbo = landA[0:1, 1024:2048]
        bo_row = gfm[64:65, 2048:3072]
        kv_bank = [0, 0, 0, 1, 1, 1, 2, 2]
        kv_col = [0, 130, 260, 0, 130, 260, 0, 130]

        wcs = CastStream(p, [landA[:, 1024 * s:1024 * (s + 1)] for s in range(3)],
                         ["sW0", "sW1", "sW2"], ["sC0", "sC1", "sC2"],
                         ["vector", "scalar", "vector"])
        ocs = CastStream(p, [landO[:, 0:1024]],
                         ["sO0"], ["sD0"], ["scalar"])

        p.dma("sync", identb[:], c_id_bf[:], "sLC")
        p.dma("sync", onesb[0:1, :], c_ones_bf[:], "sLC")
        p.dma("sync", onesb[32:33, :], c_ones_bf[:], "sLC")
        p.dma("sync", onesb[64:65, :], c_ones_bf[:], "sLC")
        p.dma("sync", cb32[0:1, :], cbias[0:1, :], "sLC")
        p.dma("sync", cb32[32:33, :], cbias[1:2, :], "sLC")
        p.dma("sync", cb32[64:65, :], cbias[2:3, :], "sLC")
        p.dma("sync", cbo[:], cbias[3:4, :], "sLC")
        for c in range(8):
            p.dma("sync", routerf[:, 8 * c:8 * c + 8], router[128 * c:128 * (c + 1), :], "sLC")
        vLC = p.c["sLC"]
        p.wait("vector", "sLC", vLC)
        p.op("vector", lambda E: E.tensor_copy(out=biasr[0:1, 0:1024], in_=cb32[0:1, :]), inc="sVE")
        p.op("vector", lambda E: E.tensor_copy(out=biasr[32:33, 0:1024], in_=cb32[32:33, :]), inc="sVE")
        p.op("vector", lambda E: E.tensor_copy(out=biasr[64:65, 0:1024], in_=cb32[64:65, :]), inc="sVE")
        p.op("vector", lambda E: E.tensor_copy(out=bo_row, in_=cbo), inc="sVE")
        vBIAS = p.op("vector", lambda E: E.tensor_copy(out=routerb[:], in_=routerf[:]), inc="sVE")
        p.op("gpsimd", lambda E: E.memset(kvb[:], 0.0), inc="sPO")
        p.op("gpsimd", lambda E: E.memset(ksel[:], 0.0), inc="sPO")
        for s in range(2):
            p.op("gpsimd",
                 lambda E, s=s: E.memset(vxs[s].rearrange("p (g c) -> p g c", c=65)[:, :, 64:65], 1.0),
                 inc="sPO")
        vMEMS = p.c["sPO"]

        def load_attn_weights(full):
            mats = ((wkT, wbfA[:, 8192:16384]), (wvT, wbfB[:, 0:8192])) if full else ()
            mats = mats + ((wqT, wbfA[:, 0:8192]), (woT, wbfB[:, 8192:16384]))
            for src, dst in mats:
                for d in range(8):
                    wcs.load(src[128 * d:128 * (d + 1), :], dst[:, 1024 * d:1024 * (d + 1)])

        def load_expert_insw(e):
            for hc in range(2):
                for mat_e, mat_s, dst in ((w_in_e, w_in_s, wbfA), (w_sw_e, w_sw_s, wbfB)):
                    for d in range(8):
                        if e < NE:
                            sl = mat_e[e, 128 * d:128 * (d + 1), 1024 * hc:1024 * (hc + 1)]
                        else:
                            sl = mat_s[128 * d:128 * (d + 1), 1024 * hc:1024 * (hc + 1)]
                        wcs.load(sl, dst[:, 2048 * d + 1024 * hc:2048 * d + 1024 * (hc + 1)])

        def load_expert_wout(e):
            for f in range(16):
                sl = (w_out_e[e, 128 * f:128 * (f + 1), :] if e < NE
                      else w_out_s[128 * f:128 * (f + 1), :])
                ocs.load(sl, wod[:, 1024 * f:1024 * (f + 1)])

        def ln_stats(xtile_ap, stats_ap):
            p.gop("vector", lambda E: E.bn_stats(out=bn12[:, 0:6], in_=xtile_ap[:, 0:512]))
            p.gop("vector", lambda E: E.bn_stats(out=bn12[:, 6:12], in_=xtile_ap[:, 512:1024]))
            p.gop("vector", lambda E: E.bn_aggr(out=mvv[:], in_=bn12[:]))
            p.gop("vector", lambda E: E.tensor_scalar(
                out=stats_ap[:, 0:1], in0=mvv[:, 0:1], scalar1=-1.0, scalar2=None, op0=ALU.mult))
            vv = p.gop("vector", lambda E: E.tensor_scalar(
                out=mvv[:, 1:2], in0=mvv[:, 1:2], scalar1=LNEPS, scalar2=None, op0=ALU.add))
            p.wait("scalar", "sVE", vv)
            va = p.gop("scalar", lambda E: E.activation(out=mvv[:, 1:2], in_=mvv[:, 1:2], func=AF.Ln))
            va = p.gop("scalar", lambda E, s=stats_ap: E.activation(
                out=s[:, 1:2], in_=mvv[:, 1:2], func=AF.Exp, scale=-0.5))
            p.wait("vector", "sAC", va)

        p.wait("sync", "sVE", vBIAS)
        load_attn_weights(full=True)
        cw_kv = wcs.drain(16)
        wcs.drain()
        kv_mark = {}
        vn_mark = {}
        for t in range(NT):
            if t < 8:
                xt = xh[:, 1024 * t:1024 * (t + 1)]
            else:
                xt = xa32[:, 1024 * (t % 2):1024 * (t % 2 + 1)]
            sx = "sLXa" if t % 2 == 0 else "sLXb"
            if t >= 2:
                p.wait("sync", "sVE", vn_mark[t - 2])
            p.dma("sync", xt, x_in[128 * t:128 * (t + 1), :], sx)
            p.wait("vector", sx, p.c[sx])
            stat = st1[:, 2 * t:2 * t + 2]
            ln_stats(xt, stat)
            lt = lnts[t % 2]
            vn = p.op("vector", lambda E, xt=xt, lt=lt, s=stat: E.tensor_scalar(
                out=lt, in0=xt, scalar1=s[:, 0:1], scalar2=s[:, 1:2],
                op0=ALU.add, op1=ALU.mult), inc="sVE")
            vn_mark[t] = vn
            p.wait("tensor", "sVE", vn)
            for c in range(8):
                vtp = p.op("tensor", lambda E, c=c, lt=lt: E.transpose(
                    out=bank_bf(3)[:, 128 * c:128 * (c + 1)], in_=lt[:, 128 * c:128 * (c + 1)],
                    identity=identb[:]), inc="sPE" if c == 7 else None)
            p.wait("vector", "sPE", vtp)
            vev = p.op("vector", lambda E: E.tensor_copy(out=aT, in_=bank_bf(3)[:]), inc="sVE")

            p.wait("tensor", "sVE", vev)
            if t == 0:
                p.wait_marks("tensor", cw_kv)
                p.wait("tensor", "sVE", vBIAS)
                p.wait("tensor", "sPO", vMEMS)
            for h2 in range(2):
                for d in range(8):
                    p.op("tensor", lambda E, d=d, h2=h2: E.matmul(
                        out=bank(4 + h2), lhsT=aT[:, 128 * d:128 * (d + 1)],
                        rhs=wbfA[:, 8192 + 1024 * d + 512 * h2:8192 + 1024 * d + 512 * (h2 + 1)],
                        start=(d == 0), stop=False))
                p.op("tensor", lambda E, h2=h2: E.matmul(
                    out=bank(4 + h2), lhsT=onesb[32:33, 0:128],
                    rhs=biasr[32:33, 512 * h2:512 * (h2 + 1)], start=False, stop=True))
                for d in range(8):
                    p.op("tensor", lambda E, d=d, h2=h2: E.matmul(
                        out=bank(6 + h2), lhsT=aT[:, 128 * d:128 * (d + 1)],
                        rhs=wbfB[:, 1024 * d + 512 * h2:1024 * d + 512 * (h2 + 1)],
                        start=(d == 0), stop=False))
                vkv = p.op("tensor", lambda E, h2=h2: E.matmul(
                    out=bank(6 + h2), lhsT=onesb[64:65, 0:128],
                    rhs=biasr[64:65, 512 * h2:512 * (h2 + 1)], start=False, stop=True),
                    inc="sPE" if h2 == 1 else None)

            kx, vxt, scr = kxs[t % 2], vxs[t % 2], elu_scr[t % 2]
            p.wait("vector", "sPE", vkv)
            if t >= 2:
                p.wait("vector", "sPE", kv_mark[t - 2])
            v1 = p.op("vector", lambda E, scr=scr: E.tensor_single_scalar(
                out=scr, in_=PS[:, 2048:3072], scalar=0.0, op=ALU.min), inc="sVE")
            p.wait("scalar", "sVE", v1)
            v2 = p.op("scalar", lambda E, scr=scr: E.activation(out=scr, in_=scr, func=AF.Exp),
                      inc="sAC")
            p.wait("vector", "sAC", v2)
            p.op("vector", lambda E, kx=kx, scr=scr: E.scalar_tensor_tensor(
                out=kx, in0=PS[:, 2048:3072], scalar=0.0, in1=scr,
                op0=ALU.max, op1=ALU.add), inc="sVE")
            vv2 = p.op("vector", lambda E, vxt=vxt: E.tensor_copy(
                out=vxt.rearrange("p (g c) -> p g c", c=65)[:, :, 0:64],
                in_=PS[:, 3072:4096].rearrange("p (g c) -> p g c", c=64)), inc="sVE")

            p.wait("tensor", "sVE", vv2)
            for pr in range(NPAIR):
                b, co = kv_bank[pr], kv_col[pr]
                st_ = (t == 0) and (pr in (0, 3, 6))
                sp_ = (t == NT - 1) and (pr in (2, 5, 7))
                kv_mark[t] = p.op("tensor", lambda E, pr=pr, b=b, co=co, kx=kx, vxt=vxt,
                                  st_=st_, sp_=sp_: E.matmul(
                    out=PS[:, 512 * b + co:512 * b + co + 130],
                    lhsT=kx[:, 128 * pr:128 * (pr + 1)],
                    rhs=vxt[:, 130 * pr:130 * (pr + 1)],
                    start=st_, stop=sp_), inc="sPE" if pr == NPAIR - 1 else None)

        p.wait("vector", "sPE", kv_mark[NT - 1])
        for pr in range(NPAIR):
            base = 512 * kv_bank[pr] + kv_col[pr]
            p.op("vector", lambda E, pr=pr, base=base: E.tensor_copy(
                out=kvb[0:64, 128 * pr:128 * pr + 64], in_=PS[0:64, base:base + 64]), inc="sVE")
            p.op("vector", lambda E, pr=pr, base=base: E.tensor_copy(
                out=kvb[64:128, 128 * pr + 64:128 * pr + 128],
                in_=PS[64:128, base + 65:base + 129]), inc="sVE")
            p.op("vector", lambda E, pr=pr, base=base: E.tensor_copy(
                out=ksel[0:64, 18 * pr:18 * pr + 1], in_=PS[0:64, base + 64:base + 65]), inc="sVE")
            p.op("vector", lambda E, pr=pr, base=base: E.tensor_copy(
                out=ksel[64:128, 18 * pr + 1:18 * pr + 2],
                in_=PS[64:128, base + 129:base + 130]), inc="sVE")
        vKVB = p.c["sVE"]

        for hf in range(2):
            toff = NTH * hf
            if hf == 1:
                p.wait("sync", "sPO", p.c["sPO"])
                p.wait("sync", "sVE", p.c["sVE"])
                p.wait("sync", "sPE", p.c["sPE"])
            p.dma("sync", selp[:], c_selp[:], "sSELP")
            vSELP = p.c["sSELP"]
            if hf == 1:
                for i in range(NTH):
                    t = toff + i
                    p.dma("sync", xh[:, 1024 * i:1024 * (i + 1)],
                          x_in[128 * t:128 * (t + 1), :], "sLXa" if i % 2 == 0 else "sLXb")
                load_attn_weights(full=False)
            vXHa, vXHb = p.c["sLXa"], p.c["sLXb"]

            dstf = ("sPE", p.c["sPE"]) if hf == 1 else None
            load_expert_insw(0)
            load_expert_wout(0)

            p.wait("vector", "sLXa", vXHa)
            p.wait("vector", "sLXb", vXHb)
            for i in range(NTH):
                t = toff + i
                stat = st1[:, 2 * t:2 * t + 2]
                lt = lnts[i % 2]
                vn = p.op("vector", lambda E, i=i, lt=lt, s=stat: E.tensor_scalar(
                    out=lt, in0=xh[:, 1024 * i:1024 * (i + 1)], scalar1=s[:, 0:1],
                    scalar2=s[:, 1:2], op0=ALU.add, op1=ALU.mult), inc="sVE")
                p.wait("tensor", "sVE", vn)
                bb = 3 if i % 2 == 0 else 1
                for c in range(8):
                    vtp = p.op("tensor", lambda E, c=c, lt=lt, bb=bb: E.transpose(
                        out=bank_bf(bb)[:, 128 * c:128 * (c + 1)],
                        in_=lt[:, 128 * c:128 * (c + 1)], identity=identb[:]),
                        inc="sPE" if c == 7 else None)
                p.wait("vector", "sPE", vtp)
                p.op("vector", lambda E, i=i, bb=bb: E.tensor_copy(
                    out=lnT.rearrange("p (c ti q) -> p c ti q", c=8, ti=8)[:, :, i, :],
                    in_=bank_bf(bb)[:].rearrange("p (c q) -> p c q", c=8)), inc="sVE")
                if hf == 1:
                    wcs.drain(2, dst_free=dstf)
            vLN1T = p.c["sVE"]
            cwA = [(cs, p.c[cs]) for cs in wcs.cast_sems]

            p.wait("tensor", "sVE", vLN1T)
            p.wait_marks("tensor", cwA)
            vq_evac = {}
            for o in range(8):
                g0 = 0 if o % 2 == 0 else 4
                if o >= 2:
                    p.wait("tensor", "sVE", vq_evac[o - 2])
                for d in range(8):
                    for h2 in range(2):
                        p.op("tensor", lambda E, o=o, d=d, h2=h2, g0=g0: E.matmul(
                            out=bank(g0 + h2),
                            lhsT=wbfA[:, 1024 * d + 128 * o:1024 * d + 128 * o + 128],
                            rhs=lnT[:, 1024 * d + 512 * h2:1024 * d + 512 * (h2 + 1)],
                            start=(d == 0), stop=False))
                for h2 in range(2):
                    vq = p.op("tensor", lambda E, o=o, h2=h2, g0=g0: E.matmul(
                        out=bank(g0 + h2), lhsT=biasr[0:1, 128 * o:128 * o + 128],
                        rhs=onesb[0:1, 0:512], start=False, stop=True),
                        inc="sPE" if h2 == 1 else None)
                scr = elu_scr[o % 2]
                p.wait("vector", "sPE", vq)
                v1 = p.op("vector", lambda E, scr=scr, g0=g0: E.tensor_single_scalar(
                    out=scr, in_=PS[:, 512 * g0:512 * g0 + 1024], scalar=0.0, op=ALU.min),
                    inc="sVE")
                p.wait("scalar", "sVE", v1)
                v2 = p.op("scalar", lambda E, scr=scr: E.activation(
                    out=scr, in_=scr, func=AF.Exp), inc="sAC")
                p.wait("vector", "sAC", v2)
                vq_evac[o] = p.op("vector", lambda E, o=o, scr=scr, g0=g0: E.scalar_tensor_tensor(
                    out=qh[:, 1024 * o:1024 * (o + 1)], in0=PS[:, 512 * g0:512 * g0 + 1024],
                    scalar=0.0, in1=scr, op0=ALU.max, op1=ALU.add), inc="sVE")
            p.wait("tensor", "sVE", vq_evac[7])

            p.wait("tensor", "sVE", vKVB)
            for pr in range(8):
                for h2 in range(2):
                    vZm = p.op("tensor", lambda E, pr=pr, h2=h2: E.matmul(
                        out=PS[0:16, 512 * (6 + h2):512 * (6 + h2) + 512],
                        lhsT=ksel[:, 16 * pr:16 * (pr + 1)],
                        rhs=qh[:, 1024 * pr + 512 * h2:1024 * pr + 512 * (h2 + 1)],
                        start=(pr == 0), stop=(pr == 7)),
                        inc="sPE" if (pr == 7 and h2 == 1) else None)
            p.wait("vector", "sPE", vZm)
            p.gop("vector", lambda E: E.tensor_scalar(
                out=zr1[:, 0:512], in0=PS[0:16, 3072:3584], scalar1=ZEPS, scalar2=None, op0=ALU.add))
            p.gop("vector", lambda E: E.tensor_scalar(
                out=zr1[:, 512:1024], in0=PS[0:16, 3584:4096], scalar1=ZEPS, scalar2=None, op0=ALU.add))
            p.gop("vector", lambda E: E.reciprocal(out=zr2[:], in_=zr1[:]))
            vZR = p.c["sVE"]

            p.wait("tensor", "sVE", vZR)
            p.wait("tensor", "sSELP", vSELP)
            last_evac = {}
            for pr in range(8):
                ob = 0 if pr % 2 == 0 else 4
                zb_b = 2 if pr % 2 == 0 else 6
                if pr >= 2:
                    p.wait("tensor", "sVE", last_evac[pr - 2])
                for h2 in range(2):
                    p.op("tensor", lambda E, pr=pr, h2=h2, ob=ob: E.matmul(
                        out=bank(ob + h2), lhsT=kvb[:, 128 * pr:128 * (pr + 1)],
                        rhs=qh[:, 1024 * pr + 512 * h2:1024 * pr + 512 * (h2 + 1)],
                        start=True, stop=True))
                    vs2 = p.op("tensor", lambda E, pr=pr, h2=h2, zb_b=zb_b: E.matmul(
                        out=bank(zb_b + h2), lhsT=selp[:, 128 * pr:128 * (pr + 1)],
                        rhs=zr2[:, 512 * h2:512 * (h2 + 1)], start=True, stop=True),
                        inc="sPE" if h2 == 1 else None)
                p.wait("scalar", "sPE", vs2)
                if pr >= 1:
                    p.wait("scalar", "sVE", last_evac[pr - 1])
                va = p.op("scalar", lambda E, zb_b=zb_b: E.copy(
                    out=zbuf, in_=PS[:, 512 * zb_b:512 * zb_b + 1024]), inc="sAC")
                p.wait("vector", "sAC", va)
                last_evac[pr] = p.op("vector", lambda E, pr=pr, ob=ob: E.tensor_tensor(
                    out=lnT[:, 1024 * pr:1024 * (pr + 1)],
                    in0=PS[:, 512 * ob:512 * ob + 1024], in1=zbuf, op=ALU.mult), inc="sVE")
            vATT = p.c["sVE"]

            p.wait("tensor", "sVE", vATT)
            last_x = {}
            for i in range(NTH):
                g0 = 0 if i % 2 == 0 else 4
                if i >= 2:
                    p.wait("tensor", "sVE", last_x[i - 2])
                for d in range(8):
                    for h2 in range(2):
                        p.op("tensor", lambda E, i=i, d=d, h2=h2, g0=g0: E.matmul(
                            out=bank(g0 + h2),
                            lhsT=lnT[:, 1024 * d + 128 * i:1024 * d + 128 * i + 128],
                            rhs=wbfB[:, 8192 + 1024 * d + 512 * h2:8192 + 1024 * d + 512 * (h2 + 1)],
                            start=(d == 0), stop=False))
                for h2 in range(2):
                    vo = p.op("tensor", lambda E, h2=h2, g0=g0: E.matmul(
                        out=bank(g0 + h2), lhsT=onesb[64:65, 0:128],
                        rhs=bo_row[:, 512 * h2:512 * (h2 + 1)],
                        start=False, stop=True), inc="sPE" if h2 == 1 else None)
                p.wait("vector", "sPE", vo)
                last_x[i] = p.op("vector", lambda E, i=i, g0=g0: E.scalar_tensor_tensor(
                    out=xh[:, 1024 * i:1024 * (i + 1)], in0=PS[:, 512 * g0:512 * g0 + 1024],
                    scalar=0.0, in1=xh[:, 1024 * i:1024 * (i + 1)],
                    op0=ALU.add, op1=ALU.add), inc="sVE")
            vXATT = p.c["sVE"]

            dfree = ("sPE", p.c["sPE"])
            mark_hc0 = mark_hc1 = None

            p.wait("vector", "sVE", vXATT)
            for i in range(NTH):
                p.op("vector", lambda E, i=i: E.bn_stats(
                    out=bn12w[:, 12 * i:12 * i + 6], in_=xh[:, 1024 * i:1024 * i + 512]))
                p.op("vector", lambda E, i=i: E.bn_stats(
                    out=bn12w[:, 12 * i + 6:12 * i + 12],
                    in_=xh[:, 1024 * i + 512:1024 * (i + 1)]),
                    inc="sVE" if i == NTH - 1 else None)
            p.wait("vector", "sVE", p.c["sVE"])
            for i in range(NTH):
                p.op("vector", lambda E, i=i: E.bn_aggr(
                    out=st2[:, 2 * i:2 * i + 2], in_=bn12w[:, 12 * i:12 * i + 12]),
                    inc="sVE" if i == NTH - 1 else None)
            p.wait("vector", "sVE", p.c["sVE"])
            sigb = mskb
            nmv = st2.rearrange("p (t two) -> p t two", two=2)
            p.gop("vector", lambda E: E.tensor_scalar(
                out=nmv[:, :, 0:1], in0=nmv[:, :, 0:1], scalar1=-1.0, scalar2=None, op0=ALU.mult))
            vv_ = p.gop("vector", lambda E: E.tensor_scalar(
                out=nmv[:, :, 1:2], in0=nmv[:, :, 1:2], scalar1=LNEPS, scalar2=None, op0=ALU.add))
            p.wait("scalar", "sVE", vv_)
            p.gop("scalar", lambda E: E.activation(
                out=nmv[:, :, 1:2], in_=nmv[:, :, 1:2], func=AF.Ln))
            va_ = p.gop("scalar", lambda E: E.activation(
                out=nmv[:, :, 1:2], in_=nmv[:, :, 1:2], func=AF.Exp, scale=-0.5))
            p.wait("vector", "sAC", va_)
            for i in range(NTH):
                stat = st2[:, 2 * i:2 * i + 2]
                lt = lnts[i % 2]
                vn = p.op("vector", lambda E, i=i, lt=lt, s=stat: E.tensor_scalar(
                    out=lt, in0=xh[:, 1024 * i:1024 * (i + 1)], scalar1=s[:, 0:1],
                    scalar2=s[:, 1:2], op0=ALU.add, op1=ALU.mult), inc="sVE")
                p.wait("tensor", "sVE", vn)
                bb = 3 if i % 2 == 0 else 1
                sb_ = 2 if i % 2 == 0 else 0
                for c in range(8):
                    vtp = p.op("tensor", lambda E, c=c, lt=lt, bb=bb: E.transpose(
                        out=bank_bf(bb)[:, 128 * c:128 * (c + 1)],
                        in_=lt[:, 128 * c:128 * (c + 1)], identity=identb[:]),
                        inc="sPE" if c == 7 else None)
                p.wait("vector", "sPE", vtp)
                vevt = p.op("vector", lambda E, i=i, bb=bb: E.tensor_copy(
                    out=lnT.rearrange("p (c ti q) -> p c ti q", c=8, ti=8)[:, :, i, :],
                    in_=bank_bf(bb)[:].rearrange("p (c q) -> p c q", c=8)), inc="sVE")
                p.wait("tensor", "sVE", vevt)
                for d in range(8):
                    vsc = p.op("tensor", lambda E, i=i, d=d, sb_=sb_: E.matmul(
                        out=PS[:, 512 * sb_:512 * sb_ + 8],
                        lhsT=lnT[:, 1024 * d + 128 * i:1024 * d + 128 * i + 128],
                        rhs=routerb[:, 8 * d:8 * (d + 1)],
                        start=(d == 0), stop=(d == 7)), inc="sPE" if d == 7 else None)
                p.wait("vector", "sPE", vsc)
                p.gop("vector", lambda E, i=i, sb_=sb_: E.tensor_copy(
                    out=scall[:, 8 * i:8 * (i + 1)], in_=PS[:, 512 * sb_:512 * sb_ + 8]))
                mk = wcs.drain(4, dst_free=dfree)
                if i == 3:
                    mark_hc0 = mk
                if i == 7:
                    mark_hc1 = mk
            sc3 = scall.rearrange("p (t e) -> p t e", e=8)
            p.gop("vector", lambda E: E.tensor_reduce(
                out=mxb[:, 0:8], in_=sc3, axis=AX.X, op=ALU.max))
            p.gop("vector", lambda E: E.tensor_tensor(
                out=mskb.rearrange("p (t e) -> p t e", e=8), in0=sc3,
                in1=mxb[:, 0:8].rearrange("p (t o) -> p t o", o=1).broadcast_to([128, 8, 8]),
                op=ALU.is_ge))
            p.gop("vector", lambda E: E.scalar_tensor_tensor(
                out=mskb[:], in0=mskb[:], scalar=-1e30, in1=scall[:], op0=ALU.mult, op1=ALU.add))
            p.gop("vector", lambda E: E.tensor_reduce(
                out=mxb[:, 8:16], in_=mskb.rearrange("p (t e) -> p t e", e=8), axis=AX.X, op=ALU.max))
            vg1 = p.gop("vector", lambda E: E.tensor_tensor(
                out=gatb.rearrange("p (t e) -> p t e", e=8), in0=sc3,
                in1=mxb[:, 8:16].rearrange("p (t o) -> p t o", o=1).broadcast_to([128, 8, 8]),
                op=ALU.is_ge))
            p.wait("scalar", "sVE", vg1)
            vsg = p.op("scalar", lambda E: E.activation(out=sigb[:], in_=scall[:], func=AF.Sigmoid),
                       inc="sAC")
            p.wait("vector", "sAC", vsg)
            vmu = p.op("vector", lambda E: E.tensor_tensor(
                out=gatb[:], in0=gatb[:], in1=sigb[:], op=ALU.mult), inc="sVE")
            p.wait("vector", "sVE", vmu)
            vga = p.gop("vector", lambda E: E.tensor_copy(out=gatbb[:], in_=gatb[:]))
            p.wait("tensor", "sVE", vga)
            vgt = p.op("tensor", lambda E: E.transpose(
                out=PS[0:64, 1024:1152].bitcast(BF16)[:, 0:128], in_=gatbb[:],
                identity=identb[:]), inc="sPE")
            p.wait("vector", "sPE", vgt)
            vgb = p.op("vector", lambda E: E.tensor_copy(
                out=gTb[:], in_=PS[0:64, 1024:1152].bitcast(BF16)[:, 0:128]), inc="sVE")
            p.wait("gpsimd", "sVE", vgb)
            p.dma("gpsimd", dgT[:, :], gTb[:], "sGF")
            p.wait("gpsimd", "sGF", p.c["sGF"])
            for e in range(NE):
                pp, cc = 32 * (e // 3), 1024 * (e % 3)
                src_rows = dgT.rearrange("(t e) c -> e t c", e=8)[e:e + 1]
                dst_row = gfm[pp:pp + 1, cc:cc + 1024].rearrange("o (t c) -> o t c", c=128)
                p.dma("gpsimd", dst_row, src_rows, "sGF")
            vGF = p.c["sGF"]
            vLN2T = p.c["sVE"]

            rt_ready = {}
            gse_mark = {}

            def prepare_rt(e, tq, war_mark):
                if e >= NE:
                    rt_ready[(e, tq)] = 0
                    gse_mark[(e, tq)] = 0
                    return
                pp, cc = 32 * (e // 3), 1024 * (e % 3)
                gs = gslot[tq]
                p.wait("tensor", "sGF", vGF)
                p.wait("tensor", "sAC", p.c["sAC"])
                vgbm = p.op("tensor", lambda E, pp=pp, cc=cc, tq=tq: E.matmul(
                    out=bank(0), lhsT=onesb[pp:pp + 1, 0:128],
                    rhs=gfm[pp:pp + 1, cc + 512 * tq:cc + 512 * (tq + 1)],
                    start=True, stop=True), inc="sPE")
                p.wait("scalar", "sPE", vgbm)
                vgse = p.op("scalar", lambda E, gs=gs: E.copy(out=gs, in_=bank(0)),
                            inc="sAC")
                p.wait("vector", "sAC", vgse)
                if war_mark:
                    p.wait("vector", "sPE", war_mark)
                for d in range(8):
                    vrt = p.op("vector", lambda E, d=d, tq=tq, gs=gs: E.tensor_tensor(
                        out=rt[:, 512 * d:512 * (d + 1)],
                        in0=lnT[:, 1024 * d + 512 * tq:1024 * d + 512 * tq + 512],
                        in1=gs, op=ALU.mult), inc="sVE" if d == 7 else None)
                rt_ready[(e, tq)] = vrt
                gse_mark[(e, tq)] = vgse

            units = [(e_, tq_) for e_ in range(NE + 1) for tq_ in range(2)]
            prepare_rt(units[0][0], units[0][1], None)
            wod_free = dfree
            wod_cur = []
            for ui, (e, tq) in enumerate(units):
                if True:
                    last_h = {}
                    f7_mark = f15_mark = None
                    for f in range(16):
                        mb = 0 if f % 2 == 0 else 2
                        sb_ = 1 if f % 2 == 0 else 3
                        if f == 0:
                            p.wait("tensor", "sVE", rt_ready[(e, tq)])
                            p.wait("tensor", "sAC", gse_mark[(e, tq)])
                        if tq == 0 and f == 0:
                            p.wait_marks("tensor", mark_hc0)
                        if tq == 0 and f == 8:
                            p.wait_marks("tensor", mark_hc1)
                        if tq == 0 and f < 8 and ocs.pending:
                            wod_cur = ocs.drain(2, dst_free=wod_free)
                        if tq == 1 and e < NE and f >= 8:
                            mark_hc0 = wcs.drain(2, dst_free=("sPE", f7_mark))
                        if f >= 2:
                            p.wait("tensor", "sVE", last_h[f - 2])
                        for d in range(8):
                            if e < NE:
                                rhs_ap = rt[:, 512 * d:512 * (d + 1)]
                            else:
                                rhs_ap = lnT[:, 1024 * d + 512 * tq:1024 * d + 512 * tq + 512]
                            p.op("tensor", lambda E, f=f, d=d, mb=mb, r=rhs_ap: E.matmul(
                                out=bank(mb),
                                lhsT=wbfA[:, 2048 * d + 128 * f:2048 * d + 128 * f + 128],
                                rhs=r, start=(d == 0), stop=(d == 7)))
                        for d in range(8):
                            if e < NE:
                                rhs_ap = rt[:, 512 * d:512 * (d + 1)]
                            else:
                                rhs_ap = lnT[:, 1024 * d + 512 * tq:1024 * d + 512 * tq + 512]
                            vmm = p.op("tensor", lambda E, f=f, d=d, sb_=sb_, r=rhs_ap: E.matmul(
                                out=bank(sb_),
                                lhsT=wbfB[:, 2048 * d + 128 * f:2048 * d + 128 * f + 128],
                                rhs=r, start=(d == 0), stop=(d == 7)),
                                inc="sPE" if d == 7 else None)
                        if f == 7:
                            f7_mark = vmm
                        if f == 15:
                            f15_mark = vmm
                        p.wait("scalar", "sPE", vmm)
                        vsi = p.op("scalar", lambda E, f=f, mb=mb: E.activation(
                            out=m2s[f % 2], in_=bank(mb), func=AF.Silu), inc="sAC")
                        p.wait("vector", "sAC", vsi)
                        last_h[f] = p.op("vector", lambda E, f=f, sb_=sb_: E.tensor_tensor(
                            out=qh[:, 512 * f:512 * (f + 1)], in0=m2s[f % 2], in1=bank(sb_),
                            op=ALU.mult), inc="sVE")
                    vH = p.c["sVE"]

                    if ui + 1 < len(units):
                        prepare_rt(units[ui + 1][0], units[ui + 1][1], f15_mark)
                    if tq == 0 and e < NE:
                        load_expert_insw(e + 1)
                        load_expert_wout(e + 1)

                    p.wait("tensor", "sVE", vH)
                    p.wait_marks("tensor", wod_cur)
                    last_xo = {}
                    for j in range(4):
                        i = 4 * tq + j
                        g0 = 4 if j % 2 == 0 else 6
                        if j >= 2:
                            p.wait("tensor", "sVE", last_xo[j - 2])
                        if tq == 1 and e < NE:
                            mark_hc1 = wcs.drain(4, dst_free=("sPE", f15_mark))
                        for f in range(16):
                            for h2 in range(2):
                                vom = p.op("tensor", lambda E, f=f, j=j, h2=h2, g0=g0: E.matmul(
                                    out=bank(g0 + h2),
                                    lhsT=qh[:, 512 * f + 128 * j:512 * f + 128 * (j + 1)],
                                    rhs=wod[:, 1024 * f + 512 * h2:1024 * f + 512 * (h2 + 1)],
                                    start=(f == 0), stop=(f == 15)),
                                    inc="sPE" if (f == 15 and h2 == 1) else None)
                        p.wait("vector", "sPE", vom)
                        last_xo[j] = p.op("vector", lambda E, i=i, g0=g0: E.scalar_tensor_tensor(
                            out=xh[:, 1024 * i:1024 * (i + 1)], in0=PS[:, 512 * g0:512 * g0 + 1024],
                            scalar=0.0, in1=xh[:, 1024 * i:1024 * (i + 1)],
                            op0=ALU.add, op1=ALU.add), inc="sVE")
                if tq == 1:
                    wod_free = ("sPE", p.c["sPE"])
            p.wait("gpsimd", "sVE", p.c["sVE"])
            for i in range(NTH):
                t = toff + i
                p.dma("gpsimd", y_out[128 * t:128 * (t + 1), :], xh[:, 1024 * i:1024 * (i + 1)],
                      "sST")
            if hf == 0:
                p.wait("sync", "sST", p.c["sST"])
        p.wait("gpsimd", "sST", p.c["sST"])

        with nc.Block() as block:
            @block.sync
            def _(E):
                for f in p.q["sync"]:
                    f(E)

            @block.tensor
            def _(E):
                for f in p.q["tensor"]:
                    f(E)

            @block.vector
            def _(E):
                for f in p.q["vector"]:
                    f(E)

            @block.scalar
            def _(E):
                for f in p.q["scalar"]:
                    f(E)

            @block.gpsimd
            def _(E):
                for f in p.q["gpsimd"]:
                    f(E)

    return nc


_CACHE = {}


def kernel(**inp):
    x = np.ascontiguousarray(np.asarray(inp["L0_attn_x"], dtype=np.float32))
    shared = {
        "wqT": np.ascontiguousarray(np.asarray(inp["L0_attn_Wq"], dtype=np.float32).T),
        "wkT": np.ascontiguousarray(np.asarray(inp["L0_attn_Wk"], dtype=np.float32).T),
        "wvT": np.ascontiguousarray(np.asarray(inp["L0_attn_Wv"], dtype=np.float32).T),
        "woT": np.ascontiguousarray(np.asarray(inp["L0_attn_Wo"], dtype=np.float32).T),
        "cbias": np.ascontiguousarray(np.stack([
            np.asarray(inp["L0_attn_bq"]), np.asarray(inp["L0_attn_bk"]),
            np.asarray(inp["L0_attn_bv"]), np.asarray(inp["L0_attn_bo"])]).astype(np.float32)),
        "router": np.ascontiguousarray(np.asarray(inp["L0_ffn_router_DE"], dtype=np.float32)),
        "w_in_e": np.ascontiguousarray(np.asarray(inp["L0_ffn_w_in_eDF"], dtype=np.float32)),
        "w_sw_e": np.ascontiguousarray(np.asarray(inp["L0_ffn_w_swiglu_eDF"], dtype=np.float32)),
        "w_out_e": np.ascontiguousarray(np.asarray(inp["L0_ffn_w_out_eFD"], dtype=np.float32)),
        "w_in_s": np.ascontiguousarray(np.asarray(inp["L0_ffn_w_in_shared_FD"], dtype=np.float32).T),
        "w_sw_s": np.ascontiguousarray(np.asarray(inp["L0_ffn_w_swiglu_FD"], dtype=np.float32).T),
        "w_out_s": np.ascontiguousarray(np.asarray(inp["L0_ffn_w_out_shared_DF"], dtype=np.float32).T),
        "c_id_bf": np.eye(128, dtype=ml_dtypes.bfloat16),
        "c_id_f": np.eye(128, dtype=np.float32),
        "c_ones_bf": np.ones((1, 512), dtype=ml_dtypes.bfloat16),
    }
    selp = np.zeros((16, 1024), dtype=np.float32)
    for pr in range(8):
        for j in range(128):
            selp[2 * pr + (j // 64), 128 * pr + j] = 1.0
    shared["c_selp"] = selp

    if "nc" not in _CACHE:
        _CACHE["nc"] = build_program()
    nc = _CACHE["nc"]
    in_maps = [dict(x=np.ascontiguousarray(x[c]), **shared) for c in range(8)]
    res = run_bass_kernel_spmd(nc, in_maps, core_ids=list(range(8)))
    return np.stack([np.asarray(r["y"], dtype=np.float32) for r in res.results], axis=0)


# version 06
# speedup vs baseline: 1.1985x; 1.1985x over previous
import contextlib
from collections import defaultdict

import numpy as np
import ml_dtypes

import concourse.bass as bass
import concourse.mybir as mybir
from concourse.bass_utils import run_bass_kernel_spmd

F32 = mybir.dt.float32
BF16 = mybir.dt.bfloat16
AF = mybir.ActivationFunctionType
ALU = mybir.AluOpType
AX = mybir.AxisListType

T, D, FF, NE = 2048, 1024, 2048, 8
NT, NTH, NPAIR = 16, 8, 8
LNEPS, ZEPS = 1e-5, 1e-6

ENG_SEM = {"tensor": "sPE", "vector": "sVE", "scalar": "sAC", "gpsimd": "sPO"}


class P:
    def __init__(self, nc):
        self.nc = nc
        self.q = {k: [] for k in ("sync", "tensor", "vector", "scalar", "gpsimd")}
        self.sem = {}
        self.c = defaultdict(int)

    def mksem(self, ctx, *names):
        for n in names:
            self.sem[n] = ctx.enter_context(self.nc.semaphore(n))

    def wait(self, eng, sem, val=None):
        v = self.c[sem] if val is None else val
        if v <= 0:
            return
        h = self.sem[sem]
        self.q[eng].append(lambda E: E.wait_ge(h, v))

    def op(self, eng, fn, inc=None, n=1):
        if inc is None:
            self.q[eng].append(fn)
            return None
        self.c[inc] += n
        h = self.sem[inc]
        self.q[eng].append(lambda E: fn(E).then_inc(h, n))
        return self.c[inc]

    def gop(self, eng, fn):
        v = self.op(eng, fn, inc=ENG_SEM[eng])
        self.wait(eng, ENG_SEM[eng], v)
        return v

    def wait_marks(self, eng, marks):
        for sem, val in marks:
            self.wait(eng, sem, val)

    def dma(self, eng, out, in_, sem):
        return self.op(eng, lambda E: E.dma_start(out=out, in_=in_), inc=sem, n=16)


class CastStream:

    def __init__(self, p, slots, slot_sems, cast_sems, engines):
        self.p, self.slots, self.slot_sems, self.engines = p, slots, slot_sems, engines
        self.cast_sems = cast_sems
        self.k = 0
        self.kc = 0
        self.pending = []

    def load(self, src_ap, dst_ap):
        s = self.k % len(self.slots)
        if self.k >= len(self.slots):
            self.p.wait("sync", self.cast_sems[s], self.k // len(self.slots))
        self.p.dma("sync", self.slots[s], src_ap, self.slot_sems[s])
        self.pending.append((self.k, dst_ap, self.p.c[self.slot_sems[s]]))
        self.k += 1

    def drain(self, count=None, dst_free=None):
        p = self.p
        todo = self.pending if count is None else self.pending[:count]
        self.pending = self.pending[len(todo):]
        for idx, dst, dval in todo:
            eng = self.engines[self.kc % len(self.engines)]
            s = idx % len(self.slots)
            p.wait(eng, self.slot_sems[s], dval)
            if dst_free is not None:
                p.wait(eng, dst_free[0], dst_free[1])
            src = self.slots[s]
            if eng == "scalar":
                p.op(eng, lambda E, d=dst, sp=src: E.copy(out=d, in_=sp),
                     inc=self.cast_sems[s])
            else:
                p.op(eng, lambda E, d=dst, sp=src: E.tensor_copy(out=d, in_=sp),
                     inc=self.cast_sems[s])
            self.kc += 1
        return [(cs, p.c[cs]) for cs in self.cast_sems]


def build_program():
    nc = bass.Bass("TRN2", target_bir_lowering=False)

    def din(name, shape, dt=F32):
        return nc.dram_tensor(name, shape, dt, kind="ExternalInput").ap()

    x_in = din("x", [T, D])
    y_out = nc.dram_tensor("y", [T, D], F32, kind="ExternalOutput").ap()
    wqT, wkT, wvT, woT = (din(n, [D, D]) for n in ("wqT", "wkT", "wvT", "woT"))
    cbias = din("cbias", [4, D])
    router = din("router", [D, NE])
    w_in_e, w_sw_e = din("w_in_e", [NE, D, FF]), din("w_sw_e", [NE, D, FF])
    w_out_e = din("w_out_e", [NE, FF, D])
    w_in_s, w_sw_s = din("w_in_s", [D, FF]), din("w_sw_s", [D, FF])
    w_out_s = din("w_out_s", [FF, D])
    c_id_bf = din("c_id_bf", [128, 128], BF16)
    c_id_f = din("c_id_f", [128, 128], F32)
    c_ones_bf = din("c_ones_bf", [1, 512], BF16)
    c_selp = din("c_selp", [16, 1024], F32)
    dgT = nc.dram_tensor("dgT", [64, 128], BF16, kind="Internal").ap()

    with contextlib.ExitStack() as ctx:
        E_ = ctx.enter_context
        xh = E_(nc.sbuf_tensor("xh", [128, 8192], F32))
        lnT = E_(nc.sbuf_tensor("lnT", [128, 8192], BF16))
        qh = E_(nc.sbuf_tensor("qh", [128, 8192], BF16))
        wbfA = E_(nc.sbuf_tensor("wbfA", [128, 16384], BF16))
        wbfB = E_(nc.sbuf_tensor("wbfB", [128, 16384], BF16))
        wod = E_(nc.sbuf_tensor("wod", [128, 16384], BF16))
        landA = E_(nc.sbuf_tensor("landA", [128, 4096], F32))
        gbuf = E_(nc.sbuf_tensor("gbuf", [128, 1024], F32))
        m2 = E_(nc.sbuf_tensor("m2", [128, 1024], F32))
        scr8 = E_(nc.sbuf_tensor("scr8", [128, 2048], F32))
        kvb = E_(nc.sbuf_tensor("kvb", [128, 1024], BF16))
        ksel = E_(nc.sbuf_tensor("ksel", [128, 128], BF16))
        biasr = E_(nc.sbuf_tensor("biasr", [65, 1024], BF16))
        onesb = E_(nc.sbuf_tensor("onesb", [65, 512], BF16))
        identb = E_(nc.sbuf_tensor("identb", [128, 128], BF16))
        routerf = E_(nc.sbuf_tensor("routerf", [128, 64], F32))
        routerb = E_(nc.sbuf_tensor("routerb", [128, 64], BF16))
        gfm = E_(nc.sbuf_tensor("gfm", [65, 3072], BF16))
        st1 = E_(nc.sbuf_tensor("st1", [128, 32], F32))
        st2 = E_(nc.sbuf_tensor("st2", [128, 16], F32))
        bn12w = E_(nc.sbuf_tensor("bn12w", [128, 96], F32))
        bn12 = bn12w[:, 0:12]
        mvv = E_(nc.sbuf_tensor("mvv", [128, 2], F32))
        scall = E_(nc.sbuf_tensor("scall", [128, 64], F32))
        mskb = E_(nc.sbuf_tensor("mskb", [128, 64], F32))
        mxb = E_(nc.sbuf_tensor("mxb", [128, 16], F32))
        gatb = E_(nc.sbuf_tensor("gatb", [128, 64], F32))
        gTb = E_(nc.sbuf_tensor("gTb", [64, 128], BF16))
        gatbb = E_(nc.sbuf_tensor("gatbb", [128, 64], BF16))
        PS = E_(nc.psum_tensor("PS", [128, 4096], F32))

        p = P(nc)
        p.mksem(ctx, "sLXa", "sLXb", "sW0", "sW1", "sW2", "sW3",
                "sC0", "sC1", "sC2", "sC3", "sLC", "sSELP", "sGF", "sST",
                "sPE", "sVE", "sAC", "sPO")

        def bank(b):
            return PS[:, 512 * b:512 * (b + 1)]

        def bank_bf(b):
            return PS[:, 512 * b:512 * (b + 1)].bitcast(BF16)

        elu_scr = [scr8[:, 0:1024], scr8[:, 1024:2048]]
        zbuf = scr8[:, 1024:2048]
        rt = scr8.bitcast(BF16)
        zr1 = m2[0:16, :]
        zr2 = scr8[0:16, 0:1024]
        selp = gbuf[0:16, :]
        gslot = [gbuf.bitcast(BF16)[:, 0:512], gbuf.bitcast(BF16)[:, 512:1024]]
        m2s = [m2[:, 0:512], m2[:, 512:1024]]
        kxs = [qh[:, 0:1024], qh[:, 1024:2048]]
        vxs = [qh[:, 2048:3088], qh[:, 3088:4128]]
        aT = qh[:, 4160:5184]
        xa32 = lnT.bitcast(F32)
        lnts = [qh[:, 5184:6208], qh[:, 6208:7232]]
        cb32 = landA[0:65, 0:1024]
        cbo = landA[0:1, 1024:2048]
        bo_row = gfm[64:65, 2048:3072]
        kv_bank = [0, 0, 0, 1, 1, 1, 2, 2]
        kv_col = [0, 130, 260, 0, 130, 260, 0, 130]

        wcs = CastStream(p, [landA[:, 1024 * s:1024 * (s + 1)] for s in range(4)],
                         ["sW0", "sW1", "sW2", "sW3"], ["sC0", "sC1", "sC2", "sC3"],
                         ["vector", "scalar"])
        ocs = wcs

        p.dma("sync", identb[:], c_id_bf[:], "sLC")
        p.dma("sync", onesb[0:1, :], c_ones_bf[:], "sLC")
        p.dma("sync", onesb[32:33, :], c_ones_bf[:], "sLC")
        p.dma("sync", onesb[64:65, :], c_ones_bf[:], "sLC")
        p.dma("sync", cb32[0:1, :], cbias[0:1, :], "sLC")
        p.dma("sync", cb32[32:33, :], cbias[1:2, :], "sLC")
        p.dma("sync", cb32[64:65, :], cbias[2:3, :], "sLC")
        p.dma("sync", cbo[:], cbias[3:4, :], "sLC")
        for c in range(8):
            p.dma("sync", routerf[:, 8 * c:8 * c + 8], router[128 * c:128 * (c + 1), :], "sLC")
        vLC = p.c["sLC"]
        p.wait("vector", "sLC", vLC)
        p.op("vector", lambda E: E.tensor_copy(out=biasr[0:1, 0:1024], in_=cb32[0:1, :]), inc="sVE")
        p.op("vector", lambda E: E.tensor_copy(out=biasr[32:33, 0:1024], in_=cb32[32:33, :]), inc="sVE")
        p.op("vector", lambda E: E.tensor_copy(out=biasr[64:65, 0:1024], in_=cb32[64:65, :]), inc="sVE")
        p.op("vector", lambda E: E.tensor_copy(out=bo_row, in_=cbo), inc="sVE")
        vBIAS = p.op("vector", lambda E: E.tensor_copy(out=routerb[:], in_=routerf[:]), inc="sVE")
        p.op("gpsimd", lambda E: E.memset(kvb[:], 0.0), inc="sPO")
        p.op("gpsimd", lambda E: E.memset(ksel[:], 0.0), inc="sPO")
        for s in range(2):
            p.op("gpsimd",
                 lambda E, s=s: E.memset(vxs[s].rearrange("p (g c) -> p g c", c=65)[:, :, 64:65], 1.0),
                 inc="sPO")
        vMEMS = p.c["sPO"]

        def load_attn_weights(full):
            mats = ((wkT, wbfA[:, 8192:16384]), (wvT, wbfB[:, 0:8192])) if full else ()
            mats = mats + ((wqT, wbfA[:, 0:8192]), (woT, wbfB[:, 8192:16384]))
            for src, dst in mats:
                for d in range(8):
                    wcs.load(src[128 * d:128 * (d + 1), :], dst[:, 1024 * d:1024 * (d + 1)])

        def load_expert_insw(e):
            for hc in range(2):
                for mat_e, mat_s, dst in ((w_in_e, w_in_s, wbfA), (w_sw_e, w_sw_s, wbfB)):
                    for d in range(8):
                        if e < NE:
                            sl = mat_e[e, 128 * d:128 * (d + 1), 1024 * hc:1024 * (hc + 1)]
                        else:
                            sl = mat_s[128 * d:128 * (d + 1), 1024 * hc:1024 * (hc + 1)]
                        wcs.load(sl, dst[:, 2048 * d + 1024 * hc:2048 * d + 1024 * (hc + 1)])

        def load_expert_wout(e):
            for f in range(16):
                sl = (w_out_e[e, 128 * f:128 * (f + 1), :] if e < NE
                      else w_out_s[128 * f:128 * (f + 1), :])
                ocs.load(sl, wod[:, 1024 * f:1024 * (f + 1)])

        def ln_stats(xtile_ap, stats_ap):
            p.gop("vector", lambda E: E.bn_stats(out=bn12[:, 0:6], in_=xtile_ap[:, 0:512]))
            p.gop("vector", lambda E: E.bn_stats(out=bn12[:, 6:12], in_=xtile_ap[:, 512:1024]))
            p.gop("vector", lambda E: E.bn_aggr(out=mvv[:], in_=bn12[:]))
            p.gop("vector", lambda E: E.tensor_scalar(
                out=stats_ap[:, 0:1], in0=mvv[:, 0:1], scalar1=-1.0, scalar2=None, op0=ALU.mult))
            vv = p.gop("vector", lambda E: E.tensor_scalar(
                out=mvv[:, 1:2], in0=mvv[:, 1:2], scalar1=LNEPS, scalar2=None, op0=ALU.add))
            p.wait("scalar", "sVE", vv)
            va = p.gop("scalar", lambda E: E.activation(out=mvv[:, 1:2], in_=mvv[:, 1:2], func=AF.Ln))
            va = p.gop("scalar", lambda E, s=stats_ap: E.activation(
                out=s[:, 1:2], in_=mvv[:, 1:2], func=AF.Exp, scale=-0.5))
            p.wait("vector", "sAC", va)

        p.wait("sync", "sVE", vBIAS)
        load_attn_weights(full=True)
        cw_kv = wcs.drain(16)
        wcs.drain()
        kv_mark = {}
        vn_mark = {}
        for t in range(NT):
            if t < 8:
                xt = xh[:, 1024 * t:1024 * (t + 1)]
            else:
                xt = xa32[:, 1024 * (t % 2):1024 * (t % 2 + 1)]
            sx = "sLXa" if t % 2 == 0 else "sLXb"
            if t >= 2:
                p.wait("sync", "sVE", vn_mark[t - 2])
            p.dma("sync", xt, x_in[128 * t:128 * (t + 1), :], sx)
            p.wait("vector", sx, p.c[sx])
            stat = st1[:, 2 * t:2 * t + 2]
            ln_stats(xt, stat)
            lt = lnts[t % 2]
            vn = p.op("vector", lambda E, xt=xt, lt=lt, s=stat: E.tensor_scalar(
                out=lt, in0=xt, scalar1=s[:, 0:1], scalar2=s[:, 1:2],
                op0=ALU.add, op1=ALU.mult), inc="sVE")
            vn_mark[t] = vn
            p.wait("tensor", "sVE", vn)
            for c in range(8):
                vtp = p.op("tensor", lambda E, c=c, lt=lt: E.transpose(
                    out=bank_bf(3)[:, 128 * c:128 * (c + 1)], in_=lt[:, 128 * c:128 * (c + 1)],
                    identity=identb[:]), inc="sPE" if c == 7 else None)
            p.wait("vector", "sPE", vtp)
            vev = p.op("vector", lambda E: E.tensor_copy(out=aT, in_=bank_bf(3)[:]), inc="sVE")

            p.wait("tensor", "sVE", vev)
            if t == 0:
                p.wait_marks("tensor", cw_kv)
                p.wait("tensor", "sVE", vBIAS)
                p.wait("tensor", "sPO", vMEMS)
            for h2 in range(2):
                for d in range(8):
                    p.op("tensor", lambda E, d=d, h2=h2: E.matmul(
                        out=bank(4 + h2), lhsT=aT[:, 128 * d:128 * (d + 1)],
                        rhs=wbfA[:, 8192 + 1024 * d + 512 * h2:8192 + 1024 * d + 512 * (h2 + 1)],
                        start=(d == 0), stop=False))
                p.op("tensor", lambda E, h2=h2: E.matmul(
                    out=bank(4 + h2), lhsT=onesb[32:33, 0:128],
                    rhs=biasr[32:33, 512 * h2:512 * (h2 + 1)], start=False, stop=True))
                for d in range(8):
                    p.op("tensor", lambda E, d=d, h2=h2: E.matmul(
                        out=bank(6 + h2), lhsT=aT[:, 128 * d:128 * (d + 1)],
                        rhs=wbfB[:, 1024 * d + 512 * h2:1024 * d + 512 * (h2 + 1)],
                        start=(d == 0), stop=False))
                vkv = p.op("tensor", lambda E, h2=h2: E.matmul(
                    out=bank(6 + h2), lhsT=onesb[64:65, 0:128],
                    rhs=biasr[64:65, 512 * h2:512 * (h2 + 1)], start=False, stop=True),
                    inc="sPE" if h2 == 1 else None)

            kx, vxt, scr = kxs[t % 2], vxs[t % 2], elu_scr[t % 2]
            p.wait("vector", "sPE", vkv)
            if t >= 2:
                p.wait("vector", "sPE", kv_mark[t - 2])
            v1 = p.op("vector", lambda E, scr=scr: E.tensor_single_scalar(
                out=scr, in_=PS[:, 2048:3072], scalar=0.0, op=ALU.min), inc="sVE")
            p.wait("scalar", "sVE", v1)
            v2 = p.op("scalar", lambda E, scr=scr: E.activation(out=scr, in_=scr, func=AF.Exp),
                      inc="sAC")
            p.wait("vector", "sAC", v2)
            p.op("vector", lambda E, kx=kx, scr=scr: E.scalar_tensor_tensor(
                out=kx, in0=PS[:, 2048:3072], scalar=0.0, in1=scr,
                op0=ALU.max, op1=ALU.add), inc="sVE")
            vv2 = p.op("vector", lambda E, vxt=vxt: E.tensor_copy(
                out=vxt.rearrange("p (g c) -> p g c", c=65)[:, :, 0:64],
                in_=PS[:, 3072:4096].rearrange("p (g c) -> p g c", c=64)), inc="sVE")

            p.wait("tensor", "sVE", vv2)
            for pr in range(NPAIR):
                b, co = kv_bank[pr], kv_col[pr]
                st_ = (t == 0) and (pr in (0, 3, 6))
                sp_ = (t == NT - 1) and (pr in (2, 5, 7))
                kv_mark[t] = p.op("tensor", lambda E, pr=pr, b=b, co=co, kx=kx, vxt=vxt,
                                  st_=st_, sp_=sp_: E.matmul(
                    out=PS[:, 512 * b + co:512 * b + co + 130],
                    lhsT=kx[:, 128 * pr:128 * (pr + 1)],
                    rhs=vxt[:, 130 * pr:130 * (pr + 1)],
                    start=st_, stop=sp_), inc="sPE" if pr == NPAIR - 1 else None)

        p.wait("vector", "sPE", kv_mark[NT - 1])
        for pr in range(NPAIR):
            base = 512 * kv_bank[pr] + kv_col[pr]
            p.op("vector", lambda E, pr=pr, base=base: E.tensor_copy(
                out=kvb[0:64, 128 * pr:128 * pr + 64], in_=PS[0:64, base:base + 64]), inc="sVE")
            p.op("vector", lambda E, pr=pr, base=base: E.tensor_copy(
                out=kvb[64:128, 128 * pr + 64:128 * pr + 128],
                in_=PS[64:128, base + 65:base + 129]), inc="sVE")
            p.op("vector", lambda E, pr=pr, base=base: E.tensor_copy(
                out=ksel[0:64, 18 * pr:18 * pr + 1], in_=PS[0:64, base + 64:base + 65]), inc="sVE")
            p.op("vector", lambda E, pr=pr, base=base: E.tensor_copy(
                out=ksel[64:128, 18 * pr + 1:18 * pr + 2],
                in_=PS[64:128, base + 129:base + 130]), inc="sVE")
        vKVB = p.c["sVE"]

        for hf in range(2):
            toff = NTH * hf
            if hf == 1:
                p.wait("sync", "sPO", p.c["sPO"])
                p.wait("sync", "sVE", p.c["sVE"])
                p.wait("sync", "sPE", p.c["sPE"])
            p.dma("sync", selp[:], c_selp[:], "sSELP")
            vSELP = p.c["sSELP"]
            if hf == 1:
                for i in range(NTH):
                    t = toff + i
                    p.dma("sync", xh[:, 1024 * i:1024 * (i + 1)],
                          x_in[128 * t:128 * (t + 1), :], "sLXa" if i % 2 == 0 else "sLXb")
                load_attn_weights(full=False)
            vXHa, vXHb = p.c["sLXa"], p.c["sLXb"]

            dstf = ("sPE", p.c["sPE"]) if hf == 1 else None
            load_expert_insw(0)
            load_expert_wout(0)

            p.wait("vector", "sLXa", vXHa)
            p.wait("vector", "sLXb", vXHb)
            for i in range(NTH):
                t = toff + i
                stat = st1[:, 2 * t:2 * t + 2]
                lt = lnts[i % 2]
                vn = p.op("vector", lambda E, i=i, lt=lt, s=stat: E.tensor_scalar(
                    out=lt, in0=xh[:, 1024 * i:1024 * (i + 1)], scalar1=s[:, 0:1],
                    scalar2=s[:, 1:2], op0=ALU.add, op1=ALU.mult), inc="sVE")
                p.wait("tensor", "sVE", vn)
                bb = 3 if i % 2 == 0 else 1
                for c in range(8):
                    vtp = p.op("tensor", lambda E, c=c, lt=lt, bb=bb: E.transpose(
                        out=bank_bf(bb)[:, 128 * c:128 * (c + 1)],
                        in_=lt[:, 128 * c:128 * (c + 1)], identity=identb[:]),
                        inc="sPE" if c == 7 else None)
                p.wait("vector", "sPE", vtp)
                p.op("vector", lambda E, i=i, bb=bb: E.tensor_copy(
                    out=lnT.rearrange("p (c ti q) -> p c ti q", c=8, ti=8)[:, :, i, :],
                    in_=bank_bf(bb)[:].rearrange("p (c q) -> p c q", c=8)), inc="sVE")
                if hf == 1:
                    wcs.drain(2, dst_free=dstf)
            vLN1T = p.c["sVE"]
            cwA = [(cs, p.c[cs]) for cs in wcs.cast_sems]

            p.wait("tensor", "sVE", vLN1T)
            p.wait_marks("tensor", cwA)
            vq_evac = {}
            for o in range(8):
                g0 = 0 if o % 2 == 0 else 4
                if o >= 2:
                    p.wait("tensor", "sVE", vq_evac[o - 2])
                for d in range(8):
                    for h2 in range(2):
                        p.op("tensor", lambda E, o=o, d=d, h2=h2, g0=g0: E.matmul(
                            out=bank(g0 + h2),
                            lhsT=wbfA[:, 1024 * d + 128 * o:1024 * d + 128 * o + 128],
                            rhs=lnT[:, 1024 * d + 512 * h2:1024 * d + 512 * (h2 + 1)],
                            start=(d == 0), stop=False))
                for h2 in range(2):
                    vq = p.op("tensor", lambda E, o=o, h2=h2, g0=g0: E.matmul(
                        out=bank(g0 + h2), lhsT=biasr[0:1, 128 * o:128 * o + 128],
                        rhs=onesb[0:1, 0:512], start=False, stop=True),
                        inc="sPE" if h2 == 1 else None)
                scr = elu_scr[o % 2]
                p.wait("vector", "sPE", vq)
                v1 = p.op("vector", lambda E, scr=scr, g0=g0: E.tensor_single_scalar(
                    out=scr, in_=PS[:, 512 * g0:512 * g0 + 1024], scalar=0.0, op=ALU.min),
                    inc="sVE")
                p.wait("scalar", "sVE", v1)
                v2 = p.op("scalar", lambda E, scr=scr: E.activation(
                    out=scr, in_=scr, func=AF.Exp), inc="sAC")
                p.wait("vector", "sAC", v2)
                vq_evac[o] = p.op("vector", lambda E, o=o, scr=scr, g0=g0: E.scalar_tensor_tensor(
                    out=qh[:, 1024 * o:1024 * (o + 1)], in0=PS[:, 512 * g0:512 * g0 + 1024],
                    scalar=0.0, in1=scr, op0=ALU.max, op1=ALU.add), inc="sVE")
            p.wait("tensor", "sVE", vq_evac[7])

            p.wait("tensor", "sVE", vKVB)
            for pr in range(8):
                for h2 in range(2):
                    vZm = p.op("tensor", lambda E, pr=pr, h2=h2: E.matmul(
                        out=PS[0:16, 512 * (6 + h2):512 * (6 + h2) + 512],
                        lhsT=ksel[:, 16 * pr:16 * (pr + 1)],
                        rhs=qh[:, 1024 * pr + 512 * h2:1024 * pr + 512 * (h2 + 1)],
                        start=(pr == 0), stop=(pr == 7)),
                        inc="sPE" if (pr == 7 and h2 == 1) else None)
            p.wait("vector", "sPE", vZm)
            p.gop("vector", lambda E: E.tensor_scalar(
                out=zr1[:, 0:512], in0=PS[0:16, 3072:3584], scalar1=ZEPS, scalar2=None, op0=ALU.add))
            p.gop("vector", lambda E: E.tensor_scalar(
                out=zr1[:, 512:1024], in0=PS[0:16, 3584:4096], scalar1=ZEPS, scalar2=None, op0=ALU.add))
            p.gop("vector", lambda E: E.reciprocal(out=zr2[:], in_=zr1[:]))
            vZR = p.c["sVE"]

            p.wait("tensor", "sVE", vZR)
            p.wait("tensor", "sSELP", vSELP)
            last_evac = {}
            for pr in range(8):
                ob = 0 if pr % 2 == 0 else 4
                zb_b = 2 if pr % 2 == 0 else 6
                if pr >= 2:
                    p.wait("tensor", "sVE", last_evac[pr - 2])
                for h2 in range(2):
                    p.op("tensor", lambda E, pr=pr, h2=h2, ob=ob: E.matmul(
                        out=bank(ob + h2), lhsT=kvb[:, 128 * pr:128 * (pr + 1)],
                        rhs=qh[:, 1024 * pr + 512 * h2:1024 * pr + 512 * (h2 + 1)],
                        start=True, stop=True))
                    vs2 = p.op("tensor", lambda E, pr=pr, h2=h2, zb_b=zb_b: E.matmul(
                        out=bank(zb_b + h2), lhsT=selp[:, 128 * pr:128 * (pr + 1)],
                        rhs=zr2[:, 512 * h2:512 * (h2 + 1)], start=True, stop=True),
                        inc="sPE" if h2 == 1 else None)
                p.wait("scalar", "sPE", vs2)
                if pr >= 1:
                    p.wait("scalar", "sVE", last_evac[pr - 1])
                va = p.op("scalar", lambda E, zb_b=zb_b: E.copy(
                    out=zbuf, in_=PS[:, 512 * zb_b:512 * zb_b + 1024]), inc="sAC")
                p.wait("vector", "sAC", va)
                last_evac[pr] = p.op("vector", lambda E, pr=pr, ob=ob: E.tensor_tensor(
                    out=lnT[:, 1024 * pr:1024 * (pr + 1)],
                    in0=PS[:, 512 * ob:512 * ob + 1024], in1=zbuf, op=ALU.mult), inc="sVE")
            vATT = p.c["sVE"]

            p.wait("tensor", "sVE", vATT)
            last_x = {}
            for i in range(NTH):
                g0 = 0 if i % 2 == 0 else 4
                if i >= 2:
                    p.wait("tensor", "sVE", last_x[i - 2])
                for d in range(8):
                    for h2 in range(2):
                        p.op("tensor", lambda E, i=i, d=d, h2=h2, g0=g0: E.matmul(
                            out=bank(g0 + h2),
                            lhsT=lnT[:, 1024 * d + 128 * i:1024 * d + 128 * i + 128],
                            rhs=wbfB[:, 8192 + 1024 * d + 512 * h2:8192 + 1024 * d + 512 * (h2 + 1)],
                            start=(d == 0), stop=False))
                for h2 in range(2):
                    vo = p.op("tensor", lambda E, h2=h2, g0=g0: E.matmul(
                        out=bank(g0 + h2), lhsT=onesb[64:65, 0:128],
                        rhs=bo_row[:, 512 * h2:512 * (h2 + 1)],
                        start=False, stop=True), inc="sPE" if h2 == 1 else None)
                p.wait("vector", "sPE", vo)
                last_x[i] = p.op("vector", lambda E, i=i, g0=g0: E.scalar_tensor_tensor(
                    out=xh[:, 1024 * i:1024 * (i + 1)], in0=PS[:, 512 * g0:512 * g0 + 1024],
                    scalar=0.0, in1=xh[:, 1024 * i:1024 * (i + 1)],
                    op0=ALU.add, op1=ALU.add), inc="sVE")
            vXATT = p.c["sVE"]

            dfree = ("sPE", p.c["sPE"])
            mark_hc0 = mark_hc1 = None

            p.wait("vector", "sVE", vXATT)
            for i in range(NTH):
                p.op("vector", lambda E, i=i: E.bn_stats(
                    out=bn12w[:, 12 * i:12 * i + 6], in_=xh[:, 1024 * i:1024 * i + 512]))
                p.op("vector", lambda E, i=i: E.bn_stats(
                    out=bn12w[:, 12 * i + 6:12 * i + 12],
                    in_=xh[:, 1024 * i + 512:1024 * (i + 1)]),
                    inc="sVE" if i == NTH - 1 else None)
            p.wait("vector", "sVE", p.c["sVE"])
            for i in range(NTH):
                p.op("vector", lambda E, i=i: E.bn_aggr(
                    out=st2[:, 2 * i:2 * i + 2], in_=bn12w[:, 12 * i:12 * i + 12]),
                    inc="sVE" if i == NTH - 1 else None)
            p.wait("vector", "sVE", p.c["sVE"])
            sigb = mskb
            nmv = st2.rearrange("p (t two) -> p t two", two=2)
            p.gop("vector", lambda E: E.tensor_scalar(
                out=nmv[:, :, 0:1], in0=nmv[:, :, 0:1], scalar1=-1.0, scalar2=None, op0=ALU.mult))
            vv_ = p.gop("vector", lambda E: E.tensor_scalar(
                out=nmv[:, :, 1:2], in0=nmv[:, :, 1:2], scalar1=LNEPS, scalar2=None, op0=ALU.add))
            p.wait("scalar", "sVE", vv_)
            p.gop("scalar", lambda E: E.activation(
                out=nmv[:, :, 1:2], in_=nmv[:, :, 1:2], func=AF.Ln))
            va_ = p.gop("scalar", lambda E: E.activation(
                out=nmv[:, :, 1:2], in_=nmv[:, :, 1:2], func=AF.Exp, scale=-0.5))
            p.wait("vector", "sAC", va_)
            for i in range(NTH):
                stat = st2[:, 2 * i:2 * i + 2]
                lt = lnts[i % 2]
                vn = p.op("vector", lambda E, i=i, lt=lt, s=stat: E.tensor_scalar(
                    out=lt, in0=xh[:, 1024 * i:1024 * (i + 1)], scalar1=s[:, 0:1],
                    scalar2=s[:, 1:2], op0=ALU.add, op1=ALU.mult), inc="sVE")
                p.wait("tensor", "sVE", vn)
                bb = 3 if i % 2 == 0 else 1
                sb_ = 2 if i % 2 == 0 else 0
                for c in range(8):
                    vtp = p.op("tensor", lambda E, c=c, lt=lt, bb=bb: E.transpose(
                        out=bank_bf(bb)[:, 128 * c:128 * (c + 1)],
                        in_=lt[:, 128 * c:128 * (c + 1)], identity=identb[:]),
                        inc="sPE" if c == 7 else None)
                p.wait("vector", "sPE", vtp)
                vevt = p.op("vector", lambda E, i=i, bb=bb: E.tensor_copy(
                    out=lnT.rearrange("p (c ti q) -> p c ti q", c=8, ti=8)[:, :, i, :],
                    in_=bank_bf(bb)[:].rearrange("p (c q) -> p c q", c=8)), inc="sVE")
                p.wait("tensor", "sVE", vevt)
                for d in range(8):
                    vsc = p.op("tensor", lambda E, i=i, d=d, sb_=sb_: E.matmul(
                        out=PS[:, 512 * sb_:512 * sb_ + 8],
                        lhsT=lnT[:, 1024 * d + 128 * i:1024 * d + 128 * i + 128],
                        rhs=routerb[:, 8 * d:8 * (d + 1)],
                        start=(d == 0), stop=(d == 7)), inc="sPE" if d == 7 else None)
                p.wait("vector", "sPE", vsc)
                p.gop("vector", lambda E, i=i, sb_=sb_: E.tensor_copy(
                    out=scall[:, 8 * i:8 * (i + 1)], in_=PS[:, 512 * sb_:512 * sb_ + 8]))
                mk = wcs.drain(6, dst_free=dfree)
                if i == 3:
                    mark_hc0 = mk
                if i == 7:
                    mark_hc1 = mk
            sc3 = scall.rearrange("p (t e) -> p t e", e=8)
            p.gop("vector", lambda E: E.tensor_reduce(
                out=mxb[:, 0:8], in_=sc3, axis=AX.X, op=ALU.max))
            p.gop("vector", lambda E: E.tensor_tensor(
                out=mskb.rearrange("p (t e) -> p t e", e=8), in0=sc3,
                in1=mxb[:, 0:8].rearrange("p (t o) -> p t o", o=1).broadcast_to([128, 8, 8]),
                op=ALU.is_ge))
            p.gop("vector", lambda E: E.scalar_tensor_tensor(
                out=mskb[:], in0=mskb[:], scalar=-1e30, in1=scall[:], op0=ALU.mult, op1=ALU.add))
            p.gop("vector", lambda E: E.tensor_reduce(
                out=mxb[:, 8:16], in_=mskb.rearrange("p (t e) -> p t e", e=8), axis=AX.X, op=ALU.max))
            vg1 = p.gop("vector", lambda E: E.tensor_tensor(
                out=gatb.rearrange("p (t e) -> p t e", e=8), in0=sc3,
                in1=mxb[:, 8:16].rearrange("p (t o) -> p t o", o=1).broadcast_to([128, 8, 8]),
                op=ALU.is_ge))
            p.wait("scalar", "sVE", vg1)
            vsg = p.op("scalar", lambda E: E.activation(out=sigb[:], in_=scall[:], func=AF.Sigmoid),
                       inc="sAC")
            p.wait("vector", "sAC", vsg)
            vmu = p.op("vector", lambda E: E.tensor_tensor(
                out=gatb[:], in0=gatb[:], in1=sigb[:], op=ALU.mult), inc="sVE")
            p.wait("vector", "sVE", vmu)
            vga = p.gop("vector", lambda E: E.tensor_copy(out=gatbb[:], in_=gatb[:]))
            p.wait("tensor", "sVE", vga)
            vgt = p.op("tensor", lambda E: E.transpose(
                out=PS[0:64, 1024:1152].bitcast(BF16)[:, 0:128], in_=gatbb[:],
                identity=identb[:]), inc="sPE")
            p.wait("vector", "sPE", vgt)
            vgb = p.op("vector", lambda E: E.tensor_copy(
                out=gTb[:], in_=PS[0:64, 1024:1152].bitcast(BF16)[:, 0:128]), inc="sVE")
            p.wait("gpsimd", "sVE", vgb)
            p.dma("gpsimd", dgT[:, :], gTb[:], "sGF")
            p.wait("gpsimd", "sGF", p.c["sGF"])
            for e in range(NE):
                pp, cc = 32 * (e // 3), 1024 * (e % 3)
                src_rows = dgT.rearrange("(t e) c -> e t c", e=8)[e:e + 1]
                dst_row = gfm[pp:pp + 1, cc:cc + 1024].rearrange("o (t c) -> o t c", c=128)
                p.dma("gpsimd", dst_row, src_rows, "sGF")
            vGF = p.c["sGF"]
            vLN2T = p.c["sVE"]

            rt_ready = {}
            gse_mark = {}

            def prepare_rt(e, tq, war_mark):
                if e >= NE:
                    rt_ready[(e, tq)] = 0
                    gse_mark[(e, tq)] = 0
                    return
                pp, cc = 32 * (e // 3), 1024 * (e % 3)
                gs = gslot[tq]
                p.wait("tensor", "sGF", vGF)
                p.wait("tensor", "sAC", p.c["sAC"])
                vgbm = p.op("tensor", lambda E, pp=pp, cc=cc, tq=tq: E.matmul(
                    out=bank(0), lhsT=onesb[pp:pp + 1, 0:128],
                    rhs=gfm[pp:pp + 1, cc + 512 * tq:cc + 512 * (tq + 1)],
                    start=True, stop=True), inc="sPE")
                p.wait("scalar", "sPE", vgbm)
                vgse = p.op("scalar", lambda E, gs=gs: E.copy(out=gs, in_=bank(0)),
                            inc="sAC")
                p.wait("vector", "sAC", vgse)
                if war_mark:
                    p.wait("vector", "sPE", war_mark)
                for d in range(8):
                    vrt = p.op("vector", lambda E, d=d, tq=tq, gs=gs: E.tensor_tensor(
                        out=rt[:, 512 * d:512 * (d + 1)],
                        in0=lnT[:, 1024 * d + 512 * tq:1024 * d + 512 * tq + 512],
                        in1=gs, op=ALU.mult), inc="sVE" if d == 7 else None)
                rt_ready[(e, tq)] = vrt
                gse_mark[(e, tq)] = vgse

            units = [(e_, tq_) for e_ in range(NE + 1) for tq_ in range(2)]
            prepare_rt(units[0][0], units[0][1], None)
            wod_free = dfree
            wod_cur = mark_hc1
            for ui, (e, tq) in enumerate(units):
                if True:
                    last_h = {}
                    f7_mark = f15_mark = None
                    for f in range(16):
                        mb = 0 if f % 2 == 0 else 2
                        sb_ = 1 if f % 2 == 0 else 3
                        if f == 0:
                            p.wait("tensor", "sVE", rt_ready[(e, tq)])
                            p.wait("tensor", "sAC", gse_mark[(e, tq)])
                        if tq == 0 and f == 0:
                            p.wait_marks("tensor", mark_hc0)
                        if tq == 0 and f == 8:
                            p.wait_marks("tensor", mark_hc1)
                        if tq == 0 and f < 8 and ocs.pending:
                            wod_cur = ocs.drain(2, dst_free=wod_free)
                        if tq == 1 and e < NE and f >= 8:
                            mark_hc0 = wcs.drain(2, dst_free=("sPE", f7_mark))
                        if f >= 2:
                            p.wait("tensor", "sVE", last_h[f - 2])
                        for d in range(8):
                            if e < NE:
                                rhs_ap = rt[:, 512 * d:512 * (d + 1)]
                            else:
                                rhs_ap = lnT[:, 1024 * d + 512 * tq:1024 * d + 512 * tq + 512]
                            p.op("tensor", lambda E, f=f, d=d, mb=mb, r=rhs_ap: E.matmul(
                                out=bank(mb),
                                lhsT=wbfA[:, 2048 * d + 128 * f:2048 * d + 128 * f + 128],
                                rhs=r, start=(d == 0), stop=(d == 7)))
                        for d in range(8):
                            if e < NE:
                                rhs_ap = rt[:, 512 * d:512 * (d + 1)]
                            else:
                                rhs_ap = lnT[:, 1024 * d + 512 * tq:1024 * d + 512 * tq + 512]
                            vmm = p.op("tensor", lambda E, f=f, d=d, sb_=sb_, r=rhs_ap: E.matmul(
                                out=bank(sb_),
                                lhsT=wbfB[:, 2048 * d + 128 * f:2048 * d + 128 * f + 128],
                                rhs=r, start=(d == 0), stop=(d == 7)),
                                inc="sPE" if d == 7 else None)
                        if f == 7:
                            f7_mark = vmm
                        if f == 15:
                            f15_mark = vmm
                        p.wait("scalar", "sPE", vmm)
                        vsi = p.op("scalar", lambda E, f=f, mb=mb: E.activation(
                            out=m2s[f % 2], in_=bank(mb), func=AF.Silu), inc="sAC")
                        p.wait("vector", "sAC", vsi)
                        last_h[f] = p.op("vector", lambda E, f=f, sb_=sb_: E.tensor_tensor(
                            out=qh[:, 512 * f:512 * (f + 1)], in0=m2s[f % 2], in1=bank(sb_),
                            op=ALU.mult), inc="sVE")
                    vH = p.c["sVE"]

                    if ui + 1 < len(units):
                        prepare_rt(units[ui + 1][0], units[ui + 1][1], f15_mark)
                    if tq == 0 and e < NE:
                        load_expert_insw(e + 1)
                        load_expert_wout(e + 1)

                    p.wait("tensor", "sVE", vH)
                    p.wait_marks("tensor", wod_cur)
                    last_xo = {}
                    for j in range(4):
                        i = 4 * tq + j
                        g0 = 4 if j % 2 == 0 else 6
                        if j >= 2:
                            p.wait("tensor", "sVE", last_xo[j - 2])
                        if tq == 1 and e < NE:
                            mark_hc1 = wcs.drain(4, dst_free=("sPE", f15_mark))
                        for f in range(16):
                            for h2 in range(2):
                                vom = p.op("tensor", lambda E, f=f, j=j, h2=h2, g0=g0: E.matmul(
                                    out=bank(g0 + h2),
                                    lhsT=qh[:, 512 * f + 128 * j:512 * f + 128 * (j + 1)],
                                    rhs=wod[:, 1024 * f + 512 * h2:1024 * f + 512 * (h2 + 1)],
                                    start=(f == 0), stop=(f == 15)),
                                    inc="sPE" if (f == 15 and h2 == 1) else None)
                        p.wait("vector", "sPE", vom)
                        last_xo[j] = p.op("vector", lambda E, i=i, g0=g0: E.scalar_tensor_tensor(
                            out=xh[:, 1024 * i:1024 * (i + 1)], in0=PS[:, 512 * g0:512 * g0 + 1024],
                            scalar=0.0, in1=xh[:, 1024 * i:1024 * (i + 1)],
                            op0=ALU.add, op1=ALU.add), inc="sVE")
                if tq == 1:
                    wod_free = ("sPE", p.c["sPE"])
            p.wait("gpsimd", "sVE", p.c["sVE"])
            for i in range(NTH):
                t = toff + i
                p.dma("gpsimd", y_out[128 * t:128 * (t + 1), :], xh[:, 1024 * i:1024 * (i + 1)],
                      "sST")
            if hf == 0:
                p.wait("sync", "sST", p.c["sST"])
        p.wait("gpsimd", "sST", p.c["sST"])

        with nc.Block() as block:
            @block.sync
            def _(E):
                for f in p.q["sync"]:
                    f(E)

            @block.tensor
            def _(E):
                for f in p.q["tensor"]:
                    f(E)

            @block.vector
            def _(E):
                for f in p.q["vector"]:
                    f(E)

            @block.scalar
            def _(E):
                for f in p.q["scalar"]:
                    f(E)

            @block.gpsimd
            def _(E):
                for f in p.q["gpsimd"]:
                    f(E)

    return nc


_CACHE = {}


def kernel(**inp):
    x = np.ascontiguousarray(np.asarray(inp["L0_attn_x"], dtype=np.float32))
    shared = {
        "wqT": np.ascontiguousarray(np.asarray(inp["L0_attn_Wq"], dtype=np.float32).T),
        "wkT": np.ascontiguousarray(np.asarray(inp["L0_attn_Wk"], dtype=np.float32).T),
        "wvT": np.ascontiguousarray(np.asarray(inp["L0_attn_Wv"], dtype=np.float32).T),
        "woT": np.ascontiguousarray(np.asarray(inp["L0_attn_Wo"], dtype=np.float32).T),
        "cbias": np.ascontiguousarray(np.stack([
            np.asarray(inp["L0_attn_bq"]), np.asarray(inp["L0_attn_bk"]),
            np.asarray(inp["L0_attn_bv"]), np.asarray(inp["L0_attn_bo"])]).astype(np.float32)),
        "router": np.ascontiguousarray(np.asarray(inp["L0_ffn_router_DE"], dtype=np.float32)),
        "w_in_e": np.ascontiguousarray(np.asarray(inp["L0_ffn_w_in_eDF"], dtype=np.float32)),
        "w_sw_e": np.ascontiguousarray(np.asarray(inp["L0_ffn_w_swiglu_eDF"], dtype=np.float32)),
        "w_out_e": np.ascontiguousarray(np.asarray(inp["L0_ffn_w_out_eFD"], dtype=np.float32)),
        "w_in_s": np.ascontiguousarray(np.asarray(inp["L0_ffn_w_in_shared_FD"], dtype=np.float32).T),
        "w_sw_s": np.ascontiguousarray(np.asarray(inp["L0_ffn_w_swiglu_FD"], dtype=np.float32).T),
        "w_out_s": np.ascontiguousarray(np.asarray(inp["L0_ffn_w_out_shared_DF"], dtype=np.float32).T),
        "c_id_bf": np.eye(128, dtype=ml_dtypes.bfloat16),
        "c_id_f": np.eye(128, dtype=np.float32),
        "c_ones_bf": np.ones((1, 512), dtype=ml_dtypes.bfloat16),
    }
    selp = np.zeros((16, 1024), dtype=np.float32)
    for pr in range(8):
        for j in range(128):
            selp[2 * pr + (j // 64), 128 * pr + j] = 1.0
    shared["c_selp"] = selp

    if "nc" not in _CACHE:
        _CACHE["nc"] = build_program()
    nc = _CACHE["nc"]
    in_maps = [dict(x=np.ascontiguousarray(x[c]), **shared) for c in range(8)]
    res = run_bass_kernel_spmd(nc, in_maps, core_ids=list(range(8)))
    return np.stack([np.asarray(r["y"], dtype=np.float32) for r in res.results], axis=0)


# version 07
# speedup vs baseline: 1.5943x; 1.3303x over previous
import contextlib
from collections import defaultdict

import numpy as np
import ml_dtypes

import concourse.bass as bass
import concourse.mybir as mybir
from concourse.bass_utils import run_bass_kernel_spmd

F32 = mybir.dt.float32
BF16 = mybir.dt.bfloat16
AF = mybir.ActivationFunctionType
ALU = mybir.AluOpType
AX = mybir.AxisListType

T, D, FF, NE = 2048, 1024, 2048, 8
NT, NTH, NPAIR = 16, 8, 8
CAP = 640
NSLOT = NE * CAP
LNEPS, ZEPS = 1e-5, 1e-6
I32 = mybir.dt.int32

ENG_SEM = {"tensor": "sPE", "vector": "sVE", "scalar": "sAC", "gpsimd": "sPO"}


class P:
    def __init__(self, nc):
        self.nc = nc
        self.q = {k: [] for k in ("sync", "tensor", "vector", "scalar", "gpsimd")}
        self.sem = {}
        self.c = defaultdict(int)

    def mksem(self, ctx, *names):
        for n in names:
            self.sem[n] = ctx.enter_context(self.nc.semaphore(n))

    def wait(self, eng, sem, val=None):
        v = self.c[sem] if val is None else val
        if v <= 0:
            return
        h = self.sem[sem]
        self.q[eng].append(lambda E: E.wait_ge(h, v))

    def op(self, eng, fn, inc=None, n=1):
        if inc is None:
            self.q[eng].append(fn)
            return None
        self.c[inc] += n
        h = self.sem[inc]
        self.q[eng].append(lambda E: fn(E).then_inc(h, n))
        return self.c[inc]

    def gop(self, eng, fn):
        v = self.op(eng, fn, inc=ENG_SEM[eng])
        self.wait(eng, ENG_SEM[eng], v)
        return v

    def wait_marks(self, eng, marks):
        for sem, val in marks:
            self.wait(eng, sem, val)

    def dma(self, eng, out, in_, sem):
        return self.op(eng, lambda E: E.dma_start(out=out, in_=in_), inc=sem, n=16)


class CastStream:

    def __init__(self, p, slots, slot_sems, cast_sems, engines):
        self.p, self.slots, self.slot_sems, self.engines = p, slots, slot_sems, engines
        self.cast_sems = cast_sems
        self.k = 0
        self.kc = 0
        self.pending = []

    def load(self, src_ap, dst_ap):
        s = self.k % len(self.slots)
        if self.k >= len(self.slots):
            self.p.wait("sync", self.cast_sems[s], self.k // len(self.slots))
        self.p.dma("sync", self.slots[s], src_ap, self.slot_sems[s])
        self.pending.append((self.k, dst_ap, self.p.c[self.slot_sems[s]]))
        self.k += 1

    def drain(self, count=None, dst_free=None):
        p = self.p
        todo = self.pending if count is None else self.pending[:count]
        self.pending = self.pending[len(todo):]
        for idx, dst, dval in todo:
            eng = self.engines[self.kc % len(self.engines)]
            s = idx % len(self.slots)
            p.wait(eng, self.slot_sems[s], dval)
            if dst_free is not None:
                p.wait(eng, dst_free[0], dst_free[1])
            src = self.slots[s]
            if eng == "scalar":
                p.op(eng, lambda E, d=dst, sp=src: E.copy(out=d, in_=sp),
                     inc=self.cast_sems[s])
            else:
                p.op(eng, lambda E, d=dst, sp=src: E.tensor_copy(out=d, in_=sp),
                     inc=self.cast_sems[s])
            self.kc += 1
        return [(cs, p.c[cs]) for cs in self.cast_sems]


def build_program():
    nc = bass.Bass("TRN2", target_bir_lowering=False)

    def din(name, shape, dt=F32):
        return nc.dram_tensor(name, shape, dt, kind="ExternalInput").ap()

    x_in = din("x", [T, D])
    y_out = nc.dram_tensor("y", [T, D], F32, kind="ExternalOutput").ap()
    wqT, wkT, wvT, woT = (din(n, [D, D]) for n in ("wqT", "wkT", "wvT", "woT"))
    cbias = din("cbias", [4, D])
    router = din("router", [D, NE])
    w_in_e, w_sw_e = din("w_in_e", [NE, D, FF]), din("w_sw_e", [NE, D, FF])
    w_out_e = din("w_out_e", [NE, FF, D])
    w_in_s, w_sw_s = din("w_in_s", [D, FF]), din("w_sw_s", [D, FF])
    w_out_s = din("w_out_s", [FF, D])
    c_id_bf = din("c_id_bf", [128, 128], BF16)
    c_ones_bf = din("c_ones_bf", [1, 512], BF16)
    c_selp = din("c_selp", [16, 1024], F32)
    c_tri = din("c_tri", [128, 128], BF16)
    c_onesf = din("c_onesf", [1, 128], F32)
    c_tok = din("c_tok", [128, 16], I32)
    c_thr = din("c_thr", [1, 8], F32)
    c_base = din("c_base", [1, 8], F32)
    c_zidx = din("c_zidx", [1, NSLOT], I32)
    ln2d = nc.dram_tensor("ln2d", [T, D], BF16, kind="Internal").ap()
    gatd = nc.dram_tensor("gatd", [T, NE], F32, kind="Internal").ap()
    idxd = nc.dram_tensor("idxd", [NSLOT], I32, kind="Internal").ap()
    poutd = nc.dram_tensor("poutd", [NSLOT + 1, D], F32, kind="Internal").ap()
    xmid = nc.dram_tensor("xmid", [T, D], F32, kind="Internal").ap()

    with contextlib.ExitStack() as ctx:
        E_ = ctx.enter_context
        xh = E_(nc.sbuf_tensor("xh", [128, 8192], F32))
        lnT = E_(nc.sbuf_tensor("lnT", [128, 8192], BF16))
        qh = E_(nc.sbuf_tensor("qh", [128, 8192], BF16))
        wbfA = E_(nc.sbuf_tensor("wbfA", [128, 16384], BF16))
        wbfB = E_(nc.sbuf_tensor("wbfB", [128, 16384], BF16))
        wod = E_(nc.sbuf_tensor("wod", [128, 16384], BF16))
        landA = E_(nc.sbuf_tensor("landA", [128, 4096], F32))
        gbuf = E_(nc.sbuf_tensor("gbuf", [128, 1024], F32))
        m2 = E_(nc.sbuf_tensor("m2", [128, 1024], F32))
        scr8 = E_(nc.sbuf_tensor("scr8", [128, 2048], F32))
        kvb = E_(nc.sbuf_tensor("kvb", [128, 1024], BF16))
        ksel = E_(nc.sbuf_tensor("ksel", [128, 128], BF16))
        biasr = E_(nc.sbuf_tensor("biasr", [65, 1024], BF16))
        onesb = E_(nc.sbuf_tensor("onesb", [65, 512], BF16))
        identb = E_(nc.sbuf_tensor("identb", [128, 128], BF16))
        routerf = E_(nc.sbuf_tensor("routerf", [128, 64], F32))
        routerb = E_(nc.sbuf_tensor("routerb", [128, 64], BF16))
        bo_row = E_(nc.sbuf_tensor("bo_row", [1, 1024], BF16))
        tri_sb = E_(nc.sbuf_tensor("tri_sb", [128, 128], BF16))
        onesf = E_(nc.sbuf_tensor("onesf", [1, 128], F32))
        tok_sb = E_(nc.sbuf_tensor("tok_sb", [128, 16], I32))
        onescol = E_(nc.sbuf_tensor("onescol", [128, 1], BF16))
        thr_b = E_(nc.sbuf_tensor("thr_b", [128, 8], F32))
        base_row = E_(nc.sbuf_tensor("base_row", [1, 8], F32))
        posf = E_(nc.sbuf_tensor("posf", [128, 8], F32))
        posi = E_(nc.sbuf_tensor("posi", [128, 8], I32))
        idxs = E_(nc.sbuf_tensor("idxs", [128, 40], I32))
        invs = E_(nc.sbuf_tensor("invs", [128, 32], I32))
        invf = E_(nc.sbuf_tensor("invf", [128, 2], F32))
        st1 = E_(nc.sbuf_tensor("st1", [128, 32], F32))
        st2 = E_(nc.sbuf_tensor("st2", [128, 16], F32))
        bn12w = E_(nc.sbuf_tensor("bn12w", [128, 96], F32))
        bn12 = bn12w[:, 0:12]
        mvv = E_(nc.sbuf_tensor("mvv", [128, 2], F32))
        scall = E_(nc.sbuf_tensor("scall", [128, 64], F32))
        mskb = E_(nc.sbuf_tensor("mskb", [128, 64], F32))
        mxb = E_(nc.sbuf_tensor("mxb", [128, 16], F32))
        gatb = E_(nc.sbuf_tensor("gatb", [128, 64], F32))
        gatbb = E_(nc.sbuf_tensor("gatbb", [128, 64], BF16))
        PS = E_(nc.psum_tensor("PS", [128, 4096], F32))

        bc_cell = {}
        p = P(nc)
        p.mksem(ctx, "sLXa", "sLXb", "sW0", "sW1", "sW2", "sW3",
                "sC0", "sC1", "sC2", "sC3", "sLC", "sSELP", "sGF", "sIX", "sST",
                "sLNa", "sLNb", "sSC", "sGA", "sGB", "sGC", "sGD", "sPa", "sPb",
                "sPE", "sVE", "sAC", "sPO")

        def bank(b):
            return PS[:, 512 * b:512 * (b + 1)]

        def bank_bf(b):
            return PS[:, 512 * b:512 * (b + 1)].bitcast(BF16)

        elu_scr = [scr8[:, 0:1024], scr8[:, 1024:2048]]
        zbuf = scr8[:, 1024:2048]
        rt = scr8.bitcast(BF16)
        zr1 = m2[0:16, :]
        zr2 = scr8[0:16, 0:1024]
        selp = gbuf[0:16, :]
        gslot = [gbuf.bitcast(BF16)[:, 0:512], gbuf.bitcast(BF16)[:, 512:1024]]
        m2s = [m2[:, 0:512], m2[:, 512:1024]]
        kxs = [qh[:, 0:1024], qh[:, 1024:2048]]
        vxs = [qh[:, 2048:3088], qh[:, 3088:4128]]
        aT = qh[:, 4160:5184]
        xa32 = lnT.bitcast(F32)
        lnts = [qh[:, 5184:6208], qh[:, 6208:7232]]
        cb32 = landA[0:65, 0:1024]
        cbo = landA[0:1, 1024:2048]
        selm = bn12w[:, 0:64]
        kv_bank = [0, 0, 0, 1, 1, 1, 2, 2]
        kv_col = [0, 130, 260, 0, 130, 260, 0, 130]

        wcs = CastStream(p, [landA[:, 1024 * s:1024 * (s + 1)] for s in range(4)],
                         ["sW0", "sW1", "sW2", "sW3"], ["sC0", "sC1", "sC2", "sC3"],
                         ["vector", "scalar"])
        ocs = wcs

        p.dma("sync", identb[:], c_id_bf[:], "sLC")
        p.dma("sync", onesb[0:1, :], c_ones_bf[:], "sLC")
        p.dma("sync", onesb[32:33, :], c_ones_bf[:], "sLC")
        p.dma("sync", onesb[64:65, :], c_ones_bf[:], "sLC")
        p.dma("sync", cb32[0:1, :], cbias[0:1, :], "sLC")
        p.dma("sync", cb32[32:33, :], cbias[1:2, :], "sLC")
        p.dma("sync", cb32[64:65, :], cbias[2:3, :], "sLC")
        p.dma("sync", cbo[:], cbias[3:4, :], "sLC")
        for c in range(8):
            p.dma("sync", routerf[:, 8 * c:8 * c + 8], router[128 * c:128 * (c + 1), :], "sLC")
        p.dma("sync", tri_sb[:], c_tri[:], "sLC")
        p.dma("sync", onesf[:], c_onesf[:], "sLC")
        p.dma("sync", tok_sb[:], c_tok[:], "sLC")
        p.dma("sync", onescol[:], c_ones_bf[0:1, 0:128].rearrange("o p -> p o"), "sLC")
        p.dma("sync", thr_b[0:1, :], c_thr[:], "sLC")
        p.dma("sync", base_row[:], c_base[:], "sLC")
        p.dma("sync", idxd.rearrange("(a b) -> a b", a=1), c_zidx[:], "sLC")
        p.dma("sync", poutd[NSLOT:NSLOT + 1, :], c_zidx[0:1, 0:1024].bitcast(F32), "sLC")
        vLC = p.c["sLC"]
        p.wait("vector", "sLC", vLC)
        p.op("vector", lambda E: E.tensor_copy(out=biasr[0:1, 0:1024], in_=cb32[0:1, :]), inc="sVE")
        p.op("vector", lambda E: E.tensor_copy(out=biasr[32:33, 0:1024], in_=cb32[32:33, :]), inc="sVE")
        p.op("vector", lambda E: E.tensor_copy(out=biasr[64:65, 0:1024], in_=cb32[64:65, :]), inc="sVE")
        p.op("vector", lambda E: E.tensor_copy(out=bo_row[:], in_=cbo), inc="sVE")
        vBIAS = p.op("vector", lambda E: E.tensor_copy(out=routerb[:], in_=routerf[:]), inc="sVE")
        p.wait("tensor", "sLC", vLC)
        vthr = p.op("tensor", lambda E: E.matmul(
            out=PS[:, 0:8], lhsT=onesf[0:1, :], rhs=thr_b[0:1, :],
            start=True, stop=True), inc="sPE")
        p.wait("vector", "sPE", vthr)
        vthr2 = p.op("vector", lambda E: E.tensor_copy(out=thr_b[:], in_=PS[:, 0:8]), inc="sVE")
        p.op("gpsimd", lambda E: E.memset(kvb[:], 0.0), inc="sPO")
        p.op("gpsimd", lambda E: E.memset(ksel[:], 0.0), inc="sPO")
        for s in range(2):
            p.op("gpsimd",
                 lambda E, s=s: E.memset(vxs[s].rearrange("p (g c) -> p g c", c=65)[:, :, 64:65], 1.0),
                 inc="sPO")
        vMEMS = p.c["sPO"]

        def load_attn_weights(full):
            mats = ((wkT, wbfA[:, 8192:16384]), (wvT, wbfB[:, 0:8192])) if full else ()
            mats = mats + ((wqT, wbfA[:, 0:8192]), (woT, wbfB[:, 8192:16384]))
            for src, dst in mats:
                for d in range(8):
                    wcs.load(src[128 * d:128 * (d + 1), :], dst[:, 1024 * d:1024 * (d + 1)])

        def load_expert_insw(e):
            for hc in range(2):
                for mat_e, mat_s, dst in ((w_in_e, w_in_s, wbfA), (w_sw_e, w_sw_s, wbfB)):
                    for d in range(8):
                        if e < NE:
                            sl = mat_e[e, 128 * d:128 * (d + 1), 1024 * hc:1024 * (hc + 1)]
                        else:
                            sl = mat_s[128 * d:128 * (d + 1), 1024 * hc:1024 * (hc + 1)]
                        wcs.load(sl, dst[:, 2048 * d + 1024 * hc:2048 * d + 1024 * (hc + 1)])

        def load_expert_wout(e):
            for f in range(16):
                sl = (w_out_e[e, 128 * f:128 * (f + 1), :] if e < NE
                      else w_out_s[128 * f:128 * (f + 1), :])
                ocs.load(sl, wod[:, 1024 * f:1024 * (f + 1)])

        def ln_stats(xtile_ap, stats_ap):
            p.gop("vector", lambda E: E.bn_stats(out=bn12[:, 0:6], in_=xtile_ap[:, 0:512]))
            p.gop("vector", lambda E: E.bn_stats(out=bn12[:, 6:12], in_=xtile_ap[:, 512:1024]))
            p.gop("vector", lambda E: E.bn_aggr(out=mvv[:], in_=bn12[:]))
            p.gop("vector", lambda E: E.tensor_scalar(
                out=stats_ap[:, 0:1], in0=mvv[:, 0:1], scalar1=-1.0, scalar2=None, op0=ALU.mult))
            vv = p.gop("vector", lambda E: E.tensor_scalar(
                out=mvv[:, 1:2], in0=mvv[:, 1:2], scalar1=LNEPS, scalar2=None, op0=ALU.add))
            p.wait("scalar", "sVE", vv)
            va = p.gop("scalar", lambda E: E.activation(out=mvv[:, 1:2], in_=mvv[:, 1:2], func=AF.Ln))
            va = p.gop("scalar", lambda E, s=stats_ap: E.activation(
                out=s[:, 1:2], in_=mvv[:, 1:2], func=AF.Exp, scale=-0.5))
            p.wait("vector", "sAC", va)

        p.wait("sync", "sVE", vBIAS)
        load_attn_weights(full=True)
        cw_kv = wcs.drain(16)
        wcs.drain()
        kv_mark = {}
        vn_mark = {}
        for t in range(NT):
            if t < 8:
                xt = xh[:, 1024 * t:1024 * (t + 1)]
            else:
                xt = xa32[:, 1024 * (t % 2):1024 * (t % 2 + 1)]
            sx = "sLXa" if t % 2 == 0 else "sLXb"
            if t >= 2:
                p.wait("sync", "sVE", vn_mark[t - 2])
            p.dma("sync", xt, x_in[128 * t:128 * (t + 1), :], sx)
            p.wait("vector", sx, p.c[sx])
            stat = st1[:, 2 * t:2 * t + 2]
            ln_stats(xt, stat)
            lt = lnts[t % 2]
            vn = p.op("vector", lambda E, xt=xt, lt=lt, s=stat: E.tensor_scalar(
                out=lt, in0=xt, scalar1=s[:, 0:1], scalar2=s[:, 1:2],
                op0=ALU.add, op1=ALU.mult), inc="sVE")
            vn_mark[t] = vn
            p.wait("tensor", "sVE", vn)
            for c in range(8):
                vtp = p.op("tensor", lambda E, c=c, lt=lt: E.transpose(
                    out=bank_bf(3)[:, 128 * c:128 * (c + 1)], in_=lt[:, 128 * c:128 * (c + 1)],
                    identity=identb[:]), inc="sPE" if c == 7 else None)
            p.wait("vector", "sPE", vtp)
            vev = p.op("vector", lambda E: E.tensor_copy(out=aT, in_=bank_bf(3)[:]), inc="sVE")

            p.wait("tensor", "sVE", vev)
            if t == 0:
                p.wait_marks("tensor", cw_kv)
                p.wait("tensor", "sVE", vthr2)
                p.wait("tensor", "sPO", vMEMS)
            for h2 in range(2):
                for d in range(8):
                    p.op("tensor", lambda E, d=d, h2=h2: E.matmul(
                        out=bank(4 + h2), lhsT=aT[:, 128 * d:128 * (d + 1)],
                        rhs=wbfA[:, 8192 + 1024 * d + 512 * h2:8192 + 1024 * d + 512 * (h2 + 1)],
                        start=(d == 0), stop=False))
                p.op("tensor", lambda E, h2=h2: E.matmul(
                    out=bank(4 + h2), lhsT=onesb[32:33, 0:128],
                    rhs=biasr[32:33, 512 * h2:512 * (h2 + 1)], start=False, stop=True))
                for d in range(8):
                    p.op("tensor", lambda E, d=d, h2=h2: E.matmul(
                        out=bank(6 + h2), lhsT=aT[:, 128 * d:128 * (d + 1)],
                        rhs=wbfB[:, 1024 * d + 512 * h2:1024 * d + 512 * (h2 + 1)],
                        start=(d == 0), stop=False))
                vkv = p.op("tensor", lambda E, h2=h2: E.matmul(
                    out=bank(6 + h2), lhsT=onesb[64:65, 0:128],
                    rhs=biasr[64:65, 512 * h2:512 * (h2 + 1)], start=False, stop=True),
                    inc="sPE" if h2 == 1 else None)

            kx, vxt, scr = kxs[t % 2], vxs[t % 2], elu_scr[t % 2]
            p.wait("vector", "sPE", vkv)
            if t >= 2:
                p.wait("vector", "sPE", kv_mark[t - 2])
            v1 = p.op("vector", lambda E, scr=scr: E.tensor_single_scalar(
                out=scr, in_=PS[:, 2048:3072], scalar=0.0, op=ALU.min), inc="sVE")
            p.wait("scalar", "sVE", v1)
            v2 = p.op("scalar", lambda E, scr=scr: E.activation(out=scr, in_=scr, func=AF.Exp),
                      inc="sAC")
            p.wait("vector", "sAC", v2)
            p.op("vector", lambda E, kx=kx, scr=scr: E.scalar_tensor_tensor(
                out=kx, in0=PS[:, 2048:3072], scalar=0.0, in1=scr,
                op0=ALU.max, op1=ALU.add), inc="sVE")
            vv2 = p.op("vector", lambda E, vxt=vxt: E.tensor_copy(
                out=vxt.rearrange("p (g c) -> p g c", c=65)[:, :, 0:64],
                in_=PS[:, 3072:4096].rearrange("p (g c) -> p g c", c=64)), inc="sVE")

            p.wait("tensor", "sVE", vv2)
            for pr in range(NPAIR):
                b, co = kv_bank[pr], kv_col[pr]
                st_ = (t == 0) and (pr in (0, 3, 6))
                sp_ = (t == NT - 1) and (pr in (2, 5, 7))
                kv_mark[t] = p.op("tensor", lambda E, pr=pr, b=b, co=co, kx=kx, vxt=vxt,
                                  st_=st_, sp_=sp_: E.matmul(
                    out=PS[:, 512 * b + co:512 * b + co + 130],
                    lhsT=kx[:, 128 * pr:128 * (pr + 1)],
                    rhs=vxt[:, 130 * pr:130 * (pr + 1)],
                    start=st_, stop=sp_), inc="sPE" if pr == NPAIR - 1 else None)

        p.wait("vector", "sPE", kv_mark[NT - 1])
        for pr in range(NPAIR):
            base = 512 * kv_bank[pr] + kv_col[pr]
            p.op("vector", lambda E, pr=pr, base=base: E.tensor_copy(
                out=kvb[0:64, 128 * pr:128 * pr + 64], in_=PS[0:64, base:base + 64]), inc="sVE")
            p.op("vector", lambda E, pr=pr, base=base: E.tensor_copy(
                out=kvb[64:128, 128 * pr + 64:128 * pr + 128],
                in_=PS[64:128, base + 65:base + 129]), inc="sVE")
            p.op("vector", lambda E, pr=pr, base=base: E.tensor_copy(
                out=ksel[0:64, 18 * pr:18 * pr + 1], in_=PS[0:64, base + 64:base + 65]), inc="sVE")
            p.op("vector", lambda E, pr=pr, base=base: E.tensor_copy(
                out=ksel[64:128, 18 * pr + 1:18 * pr + 2],
                in_=PS[64:128, base + 129:base + 130]), inc="sVE")
        vKVB = p.c["sVE"]

        for hf in range(2):
            toff = NTH * hf
            if hf == 1:
                p.wait("sync", "sPO", p.c["sPO"])
                p.wait("sync", "sVE", p.c["sVE"])
                p.wait("sync", "sPE", p.c["sPE"])
            p.dma("sync", selp[:], c_selp[:], "sSELP")
            vSELP = p.c["sSELP"]
            if hf == 1:
                for i in range(NTH):
                    t = toff + i
                    p.dma("sync", xh[:, 1024 * i:1024 * (i + 1)],
                          x_in[128 * t:128 * (t + 1), :], "sLXa" if i % 2 == 0 else "sLXb")
                load_attn_weights(full=False)
            vXHa, vXHb = p.c["sLXa"], p.c["sLXb"]

            dstf = ("sPE", p.c["sPE"]) if hf == 1 else None
            load_expert_insw(NE)
            load_expert_wout(NE)

            p.wait("vector", "sLXa", vXHa)
            p.wait("vector", "sLXb", vXHb)
            for i in range(NTH):
                t = toff + i
                stat = st1[:, 2 * t:2 * t + 2]
                lt = lnts[i % 2]
                vn = p.op("vector", lambda E, i=i, lt=lt, s=stat: E.tensor_scalar(
                    out=lt, in0=xh[:, 1024 * i:1024 * (i + 1)], scalar1=s[:, 0:1],
                    scalar2=s[:, 1:2], op0=ALU.add, op1=ALU.mult), inc="sVE")
                p.wait("tensor", "sVE", vn)
                bb = 3 if i % 2 == 0 else 1
                for c in range(8):
                    vtp = p.op("tensor", lambda E, c=c, lt=lt, bb=bb: E.transpose(
                        out=bank_bf(bb)[:, 128 * c:128 * (c + 1)],
                        in_=lt[:, 128 * c:128 * (c + 1)], identity=identb[:]),
                        inc="sPE" if c == 7 else None)
                p.wait("vector", "sPE", vtp)
                p.op("vector", lambda E, i=i, bb=bb: E.tensor_copy(
                    out=lnT.rearrange("p (c ti q) -> p c ti q", c=8, ti=8)[:, :, i, :],
                    in_=bank_bf(bb)[:].rearrange("p (c q) -> p c q", c=8)), inc="sVE")
                if hf == 1:
                    wcs.drain(2, dst_free=dstf)
            vLN1T = p.c["sVE"]
            cwA = [(cs, p.c[cs]) for cs in wcs.cast_sems]

            p.wait("tensor", "sVE", vLN1T)
            p.wait_marks("tensor", cwA)
            vq_evac = {}
            for o in range(8):
                g0 = 0 if o % 2 == 0 else 4
                if o >= 2:
                    p.wait("tensor", "sVE", vq_evac[o - 2])
                for d in range(8):
                    for h2 in range(2):
                        p.op("tensor", lambda E, o=o, d=d, h2=h2, g0=g0: E.matmul(
                            out=bank(g0 + h2),
                            lhsT=wbfA[:, 1024 * d + 128 * o:1024 * d + 128 * o + 128],
                            rhs=lnT[:, 1024 * d + 512 * h2:1024 * d + 512 * (h2 + 1)],
                            start=(d == 0), stop=False))
                for h2 in range(2):
                    vq = p.op("tensor", lambda E, o=o, h2=h2, g0=g0: E.matmul(
                        out=bank(g0 + h2), lhsT=biasr[0:1, 128 * o:128 * o + 128],
                        rhs=onesb[0:1, 0:512], start=False, stop=True),
                        inc="sPE" if h2 == 1 else None)
                scr = elu_scr[o % 2]
                p.wait("vector", "sPE", vq)
                v1 = p.op("vector", lambda E, scr=scr, g0=g0: E.tensor_single_scalar(
                    out=scr, in_=PS[:, 512 * g0:512 * g0 + 1024], scalar=0.0, op=ALU.min),
                    inc="sVE")
                p.wait("scalar", "sVE", v1)
                v2 = p.op("scalar", lambda E, scr=scr: E.activation(
                    out=scr, in_=scr, func=AF.Exp), inc="sAC")
                p.wait("vector", "sAC", v2)
                vq_evac[o] = p.op("vector", lambda E, o=o, scr=scr, g0=g0: E.scalar_tensor_tensor(
                    out=qh[:, 1024 * o:1024 * (o + 1)], in0=PS[:, 512 * g0:512 * g0 + 1024],
                    scalar=0.0, in1=scr, op0=ALU.max, op1=ALU.add), inc="sVE")
            p.wait("tensor", "sVE", vq_evac[7])

            p.wait("tensor", "sVE", vKVB)
            for pr in range(8):
                for h2 in range(2):
                    vZm = p.op("tensor", lambda E, pr=pr, h2=h2: E.matmul(
                        out=PS[0:16, 512 * (6 + h2):512 * (6 + h2) + 512],
                        lhsT=ksel[:, 16 * pr:16 * (pr + 1)],
                        rhs=qh[:, 1024 * pr + 512 * h2:1024 * pr + 512 * (h2 + 1)],
                        start=(pr == 0), stop=(pr == 7)),
                        inc="sPE" if (pr == 7 and h2 == 1) else None)
            p.wait("vector", "sPE", vZm)
            p.gop("vector", lambda E: E.tensor_scalar(
                out=zr1[:, 0:512], in0=PS[0:16, 3072:3584], scalar1=ZEPS, scalar2=None, op0=ALU.add))
            p.gop("vector", lambda E: E.tensor_scalar(
                out=zr1[:, 512:1024], in0=PS[0:16, 3584:4096], scalar1=ZEPS, scalar2=None, op0=ALU.add))
            p.gop("vector", lambda E: E.reciprocal(out=zr2[:], in_=zr1[:]))
            vZR = p.c["sVE"]

            p.wait("tensor", "sVE", vZR)
            p.wait("tensor", "sSELP", vSELP)
            last_evac = {}
            for pr in range(8):
                ob = 0 if pr % 2 == 0 else 4
                zb_b = 2 if pr % 2 == 0 else 6
                if pr >= 2:
                    p.wait("tensor", "sVE", last_evac[pr - 2])
                for h2 in range(2):
                    p.op("tensor", lambda E, pr=pr, h2=h2, ob=ob: E.matmul(
                        out=bank(ob + h2), lhsT=kvb[:, 128 * pr:128 * (pr + 1)],
                        rhs=qh[:, 1024 * pr + 512 * h2:1024 * pr + 512 * (h2 + 1)],
                        start=True, stop=True))
                    vs2 = p.op("tensor", lambda E, pr=pr, h2=h2, zb_b=zb_b: E.matmul(
                        out=bank(zb_b + h2), lhsT=selp[:, 128 * pr:128 * (pr + 1)],
                        rhs=zr2[:, 512 * h2:512 * (h2 + 1)], start=True, stop=True),
                        inc="sPE" if h2 == 1 else None)
                p.wait("scalar", "sPE", vs2)
                if pr >= 1:
                    p.wait("scalar", "sVE", last_evac[pr - 1])
                va = p.op("scalar", lambda E, zb_b=zb_b: E.copy(
                    out=zbuf, in_=PS[:, 512 * zb_b:512 * zb_b + 1024]), inc="sAC")
                p.wait("vector", "sAC", va)
                last_evac[pr] = p.op("vector", lambda E, pr=pr, ob=ob: E.tensor_tensor(
                    out=lnT[:, 1024 * pr:1024 * (pr + 1)],
                    in0=PS[:, 512 * ob:512 * ob + 1024], in1=zbuf, op=ALU.mult), inc="sVE")
            vATT = p.c["sVE"]

            p.wait("tensor", "sVE", vATT)
            last_x = {}
            for i in range(NTH):
                g0 = 0 if i % 2 == 0 else 4
                if i >= 2:
                    p.wait("tensor", "sVE", last_x[i - 2])
                for d in range(8):
                    for h2 in range(2):
                        p.op("tensor", lambda E, i=i, d=d, h2=h2, g0=g0: E.matmul(
                            out=bank(g0 + h2),
                            lhsT=lnT[:, 1024 * d + 128 * i:1024 * d + 128 * i + 128],
                            rhs=wbfB[:, 8192 + 1024 * d + 512 * h2:8192 + 1024 * d + 512 * (h2 + 1)],
                            start=(d == 0), stop=False))
                for h2 in range(2):
                    vo = p.op("tensor", lambda E, h2=h2, g0=g0: E.matmul(
                        out=bank(g0 + h2), lhsT=onesb[0:1, 0:128],
                        rhs=bo_row[:, 512 * h2:512 * (h2 + 1)],
                        start=False, stop=True), inc="sPE" if h2 == 1 else None)
                p.wait("vector", "sPE", vo)
                last_x[i] = p.op("vector", lambda E, i=i, g0=g0: E.scalar_tensor_tensor(
                    out=xh[:, 1024 * i:1024 * (i + 1)], in0=PS[:, 512 * g0:512 * g0 + 1024],
                    scalar=0.0, in1=xh[:, 1024 * i:1024 * (i + 1)],
                    op0=ALU.add, op1=ALU.add), inc="sVE")
            vXATT = p.c["sVE"]

            dfree = ("sPE", p.c["sPE"])
            mark_hc0 = mark_hc1 = None

            p.wait("vector", "sVE", vXATT)
            for i in range(NTH):
                p.op("vector", lambda E, i=i: E.bn_stats(
                    out=bn12w[:, 12 * i:12 * i + 6], in_=xh[:, 1024 * i:1024 * i + 512]))
                p.op("vector", lambda E, i=i: E.bn_stats(
                    out=bn12w[:, 12 * i + 6:12 * i + 12],
                    in_=xh[:, 1024 * i + 512:1024 * (i + 1)]),
                    inc="sVE" if i == NTH - 1 else None)
            p.wait("vector", "sVE", p.c["sVE"])
            for i in range(NTH):
                p.op("vector", lambda E, i=i: E.bn_aggr(
                    out=st2[:, 2 * i:2 * i + 2], in_=bn12w[:, 12 * i:12 * i + 12]),
                    inc="sVE" if i == NTH - 1 else None)
            p.wait("vector", "sVE", p.c["sVE"])
            sigb = mskb
            nmv = st2.rearrange("p (t two) -> p t two", two=2)
            p.gop("vector", lambda E: E.tensor_scalar(
                out=nmv[:, :, 0:1], in0=nmv[:, :, 0:1], scalar1=-1.0, scalar2=None, op0=ALU.mult))
            vv_ = p.gop("vector", lambda E: E.tensor_scalar(
                out=nmv[:, :, 1:2], in0=nmv[:, :, 1:2], scalar1=LNEPS, scalar2=None, op0=ALU.add))
            p.wait("scalar", "sVE", vv_)
            p.gop("scalar", lambda E: E.activation(
                out=nmv[:, :, 1:2], in_=nmv[:, :, 1:2], func=AF.Ln))
            va_ = p.gop("scalar", lambda E: E.activation(
                out=nmv[:, :, 1:2], in_=nmv[:, :, 1:2], func=AF.Exp, scale=-0.5))
            p.wait("vector", "sAC", va_)
            for i in range(NTH):
                g = NTH * hf + i
                stat = st2[:, 2 * i:2 * i + 2]
                lt = lnts[i % 2]
                sln = "sLNa" if i % 2 == 0 else "sLNb"
                if i >= 2:
                    p.wait("vector", sln, p.c[sln])
                vn = p.op("vector", lambda E, i=i, lt=lt, s=stat: E.tensor_scalar(
                    out=lt, in0=xh[:, 1024 * i:1024 * (i + 1)], scalar1=s[:, 0:1],
                    scalar2=s[:, 1:2], op0=ALU.add, op1=ALU.mult), inc="sVE")
                p.wait("gpsimd", "sVE", vn)
                p.dma("gpsimd", ln2d[128 * g:128 * (g + 1), :], lt, sln)
                p.wait("tensor", "sVE", vn)
                bb = 3 if i % 2 == 0 else 1
                sb_ = 2 if i % 2 == 0 else 0
                for c in range(8):
                    vtp = p.op("tensor", lambda E, c=c, lt=lt, bb=bb: E.transpose(
                        out=bank_bf(bb)[:, 128 * c:128 * (c + 1)],
                        in_=lt[:, 128 * c:128 * (c + 1)], identity=identb[:]),
                        inc="sPE" if c == 7 else None)
                p.wait("vector", "sPE", vtp)
                vevt = p.op("vector", lambda E, i=i, bb=bb: E.tensor_copy(
                    out=lnT.rearrange("p (c ti q) -> p c ti q", c=8, ti=8)[:, :, i, :],
                    in_=bank_bf(bb)[:].rearrange("p (c q) -> p c q", c=8)), inc="sVE")
                p.wait("tensor", "sVE", vevt)
                for d in range(8):
                    vsc = p.op("tensor", lambda E, i=i, d=d, sb_=sb_: E.matmul(
                        out=PS[:, 512 * sb_:512 * sb_ + 8],
                        lhsT=lnT[:, 1024 * d + 128 * i:1024 * d + 128 * i + 128],
                        rhs=routerb[:, 8 * d:8 * (d + 1)],
                        start=(d == 0), stop=(d == 7)), inc="sPE" if d == 7 else None)
                p.wait("vector", "sPE", vsc)
                p.gop("vector", lambda E, i=i, sb_=sb_: E.tensor_copy(
                    out=scall[:, 8 * i:8 * (i + 1)], in_=PS[:, 512 * sb_:512 * sb_ + 8]))
                mk = wcs.drain(6, dst_free=dfree)
                if i == 3:
                    mark_hc0 = mk
                if i == 7:
                    mark_hc1 = mk
            p.wait("vector", "sGF", p.c["sGF"])
            sc3 = scall.rearrange("p (t e) -> p t e", e=8)
            p.gop("vector", lambda E: E.tensor_reduce(
                out=mxb[:, 0:8], in_=sc3, axis=AX.X, op=ALU.max))
            p.gop("vector", lambda E: E.tensor_tensor(
                out=mskb.rearrange("p (t e) -> p t e", e=8), in0=sc3,
                in1=mxb[:, 0:8].rearrange("p (t o) -> p t o", o=1).broadcast_to([128, 8, 8]),
                op=ALU.is_ge))
            p.gop("vector", lambda E: E.scalar_tensor_tensor(
                out=mskb[:], in0=mskb[:], scalar=-1e30, in1=scall[:], op0=ALU.mult, op1=ALU.add))
            p.gop("vector", lambda E: E.tensor_reduce(
                out=mxb[:, 8:16], in_=mskb.rearrange("p (t e) -> p t e", e=8), axis=AX.X, op=ALU.max))
            p.gop("vector", lambda E: E.tensor_tensor(
                out=gatb.rearrange("p (t e) -> p t e", e=8), in0=sc3,
                in1=mxb[:, 8:16].rearrange("p (t o) -> p t o", o=1).broadcast_to([128, 8, 8]),
                op=ALU.is_ge))
            p.gop("vector", lambda E: E.tensor_copy(out=selm, in_=gatb[:]))
            vsel = p.gop("vector", lambda E: E.tensor_copy(out=gatbb[:], in_=selm))
            vg1 = p.c["sVE"]
            p.wait("scalar", "sVE", vg1)
            vsg = p.op("scalar", lambda E: E.activation(out=sigb[:], in_=scall[:], func=AF.Sigmoid),
                       inc="sAC")
            p.wait("vector", "sAC", vsg)
            vmu = p.op("vector", lambda E: E.tensor_tensor(
                out=gatb[:], in0=gatb[:], in1=sigb[:], op=ALU.mult), inc="sVE")
            p.wait("vector", "sVE", vmu)
            vism = p.gop("vector", lambda E: E.tensor_tensor(
                out=mskb.rearrange("p (t e) -> p t e", e=8), in0=sc3,
                in1=mxb[:, 0:8].rearrange("p (t o) -> p t o", o=1).broadcast_to([128, 8, 8]),
                op=ALU.is_ge))

            rt_pe_mark = {}
            base_upd = {0: p.c["sVE"]}
            scat_mark = {}
            if hf == 0:
                p.op("gpsimd", lambda E: bc_cell.__setitem__("r", E.to_reg(NSLOT - 1)) or E.nop())

            def routing_A(i):
                rb = 4 if i % 2 == 0 else 5
                tb = 6 if i % 2 == 0 else 7
                p.wait("tensor", "sVE", base_upd[i])
                p.op("tensor", lambda E, i=i, rb=rb: E.matmul(
                    out=PS[:, 512 * rb:512 * rb + 8], lhsT=tri_sb[:],
                    rhs=gatbb[:, 8 * i:8 * (i + 1)], start=True, stop=False))
                p.op("tensor", lambda E, rb=rb: E.matmul(
                    out=PS[:, 512 * rb:512 * rb + 8], lhsT=onesf[:],
                    rhs=base_row[:], start=False, stop=True))
                rt_pe_mark[i] = p.op("tensor", lambda E, i=i, tb=tb: E.matmul(
                    out=PS[0:1, 512 * tb:512 * tb + 8], lhsT=onescol[:],
                    rhs=gatbb[:, 8 * i:8 * (i + 1)], start=True, stop=True), inc="sPE")

            def routing_B(i):
                g = NTH * hf + i
                rb = 4 if i % 2 == 0 else 5
                tb = 6 if i % 2 == 0 else 7
                ps_r = PS[:, 512 * rb:512 * rb + 8]
                p.wait("vector", "sPE", rt_pe_mark[i])
                p.wait("vector", "sSC", p.c["sSC"])
                p.gop("vector", lambda E, ps_r=ps_r: E.tensor_tensor(
                    out=mxb[:, 8:16], in0=ps_r, in1=thr_b[:], op=ALU.is_ge))
                p.gop("vector", lambda E, i=i: E.tensor_scalar(
                    out=posf[:], in0=selm[:, 8 * i:8 * (i + 1)], scalar1=-1e9, scalar2=1e9,
                    op0=ALU.mult, op1=ALU.add))
                p.gop("vector", lambda E: E.scalar_tensor_tensor(
                    out=posf[:], in0=mxb[:, 8:16], scalar=1e9, in1=posf[:],
                    op0=ALU.mult, op1=ALU.add))
                p.gop("vector", lambda E, ps_r=ps_r: E.tensor_tensor(
                    out=posf[:], in0=posf[:], in1=ps_r, op=ALU.add))
                vpos = p.gop("vector", lambda E: E.tensor_copy(out=posi[:], in_=posf[:]))
                vbase = p.gop("vector", lambda E, tb=tb: E.tensor_tensor(
                    out=base_row[:], in0=base_row[:], in1=PS[0:1, 512 * tb:512 * tb + 8],
                    op=ALU.add))
                base_upd[i + 1] = vbase
                tmp = bn12w[:, 64:72]
                p.gop("vector", lambda E, i=i, tmp=tmp: E.tensor_tensor(
                    out=tmp, in0=mskb[:, 8 * i:8 * (i + 1)], in1=posf[:], op=ALU.mult))
                p.gop("vector", lambda E, tmp=tmp: E.tensor_reduce(
                    out=invf[:, 0:1], in_=tmp, axis=AX.X, op=ALU.add))
                p.gop("vector", lambda E, i=i, tmp=tmp: E.tensor_tensor(
                    out=tmp, in0=selm[:, 8 * i:8 * (i + 1)], in1=mskb[:, 8 * i:8 * (i + 1)],
                    op=ALU.subtract))
                p.gop("vector", lambda E, tmp=tmp: E.tensor_tensor(
                    out=tmp, in0=tmp, in1=posf[:], op=ALU.mult))
                p.gop("vector", lambda E, tmp=tmp: E.tensor_reduce(
                    out=invf[:, 1:2], in_=tmp, axis=AX.X, op=ALU.add))
                p.gop("vector", lambda E: E.tensor_single_scalar(
                    out=invf[:], in_=invf[:], scalar=float(NSLOT), op=ALU.min))
                vinv = p.op("vector", lambda E, g=g: E.tensor_copy(
                    out=invs[:, 2 * g:2 * g + 2], in_=invf[:]), inc="sVE")
                p.wait("gpsimd", "sVE", vinv)
                for e_ in range(NE):
                    p.op("gpsimd", lambda E, e_=e_, g=g: E.indirect_dma_start(
                        out=idxd.rearrange("(a b) -> a b", b=1),
                        out_offset=bass.IndirectOffsetOnAxis(ap=posi[:, e_:e_ + 1], axis=0),
                        in_=tok_sb[:, g:g + 1], in_offset=None,
                        bounds_check=bc_cell["r"], oob_is_err=False), inc="sSC", n=16)
                p.dma("gpsimd", gatd[128 * g:128 * (g + 1), :], gatb[:, 8 * i:8 * (i + 1)], "sGF")
                scat_mark[i] = p.c["sSC"]

            p.wait("vector", "sLNa", p.c["sLNa"])
            p.wait("vector", "sLNb", p.c["sLNb"])
            for tq in range(2):
                last_h = {}
                f15_mark = None
                for f in range(16):
                    mb = 0 if f % 2 == 0 else 2
                    sb_ = 1 if f % 2 == 0 else 3
                    if tq == 0 and f == 0:
                        p.wait_marks("tensor", mark_hc0)
                    if tq == 0 and f == 8:
                        p.wait_marks("tensor", mark_hc1)
                    if tq == 0 and f % 2 == 0:
                        routing_A(f // 2)
                    if tq == 0 and f % 2 == 1:
                        routing_B(f // 2)
                    if f >= 2:
                        p.wait("tensor", "sVE", last_h[f - 2])
                    for d in range(8):
                        rhs_ap = lnT[:, 1024 * d + 512 * tq:1024 * d + 512 * tq + 512]
                        p.op("tensor", lambda E, f=f, d=d, mb=mb, r=rhs_ap: E.matmul(
                            out=bank(mb),
                            lhsT=wbfA[:, 2048 * d + 128 * f:2048 * d + 128 * f + 128],
                            rhs=r, start=(d == 0), stop=(d == 7)))
                    for d in range(8):
                        rhs_ap = lnT[:, 1024 * d + 512 * tq:1024 * d + 512 * tq + 512]
                        vmm = p.op("tensor", lambda E, f=f, d=d, sb_=sb_, r=rhs_ap: E.matmul(
                            out=bank(sb_),
                            lhsT=wbfB[:, 2048 * d + 128 * f:2048 * d + 128 * f + 128],
                            rhs=r, start=(d == 0), stop=(d == 7)),
                            inc="sPE" if d == 7 else None)
                    if f == 15:
                        f15_mark = vmm
                    p.wait("scalar", "sPE", vmm)
                    vsi = p.op("scalar", lambda E, f=f, mb=mb: E.activation(
                        out=m2s[f % 2], in_=bank(mb), func=AF.Silu), inc="sAC")
                    p.wait("vector", "sAC", vsi)
                    last_h[f] = p.op("vector", lambda E, f=f, sb_=sb_: E.tensor_tensor(
                        out=qh[:, 512 * f:512 * (f + 1)], in0=m2s[f % 2], in1=bank(sb_),
                        op=ALU.mult), inc="sVE")
                vH = p.c["sVE"]

                p.wait("tensor", "sVE", vH)
                last_xo = {}
                for j in range(4):
                    i = 4 * tq + j
                    g0 = 4 if j % 2 == 0 else 6
                    if j >= 2:
                        p.wait("tensor", "sVE", last_xo[j - 2])
                    if hf == 1 and tq == 1 and wcs.pending:
                        mark_hc0 = wcs.drain(4, dst_free=("sPE", f15_mark))
                    for f in range(16):
                        for h2 in range(2):
                            vom = p.op("tensor", lambda E, f=f, j=j, h2=h2, g0=g0: E.matmul(
                                out=bank(g0 + h2),
                                lhsT=qh[:, 512 * f + 128 * j:512 * f + 128 * (j + 1)],
                                rhs=wod[:, 1024 * f + 512 * h2:1024 * f + 512 * (h2 + 1)],
                                start=(f == 0), stop=(f == 15)),
                                inc="sPE" if (f == 15 and h2 == 1) else None)
                    p.wait("vector", "sPE", vom)
                    last_xo[j] = p.op("vector", lambda E, i=i, g0=g0: E.scalar_tensor_tensor(
                        out=xh[:, 1024 * i:1024 * (i + 1)], in0=PS[:, 512 * g0:512 * g0 + 1024],
                        scalar=0.0, in1=xh[:, 1024 * i:1024 * (i + 1)],
                        op0=ALU.add, op1=ALU.add), inc="sVE")
            p.wait("gpsimd", "sVE", p.c["sVE"])
            for i in range(NTH):
                t = toff + i
                p.dma("gpsimd", xmid[128 * t:128 * (t + 1), :], xh[:, 1024 * i:1024 * (i + 1)],
                      "sST")
            if hf == 0:
                p.wait("sync", "sST", p.c["sST"])

        sh_ph2_done = ("sPE", p.c["sPE"])
        p.wait("sync", "sSC", p.c["sSC"])
        for col in range(40):
            e_, j_ = divmod(col, 5)
            p.dma("sync", idxs[:, col:col + 1],
                  idxd[CAP * e_ + 128 * j_:CAP * e_ + 128 * (j_ + 1)].rearrange(
                      "(a b) -> a b", b=1), "sIX")
        vIX = p.c["sIX"]
        load_expert_insw(0)
        load_expert_wout(0)

        pk_raw = [scr8.bitcast(BF16)[:, 1024 * s:1024 * (s + 1)] for s in range(2)]
        xh_bf = xh.bitcast(BF16)
        pk_sc = [xh_bf[:, 4096 + 1024 * s:4096 + 1024 * (s + 1)] for s in range(5)]
        m2sp = [m2[:, 0:640], gbuf[:, 0:640]]
        pout = [xh[:, 0:1024], xh[:, 1024:2048]]

        def h_ap(f, c0, c1):
            if f < 12:
                return lnT[:, 640 * f + c0:640 * f + c1]
            return qh[:, 5120 + 640 * (f - 12) + c0:5120 + 640 * (f - 12) + c1]

        def rt_ap(d, c0, c1):
            return qh[:, 640 * d + c0:640 * d + c1]

        gfull = p.c["sGF"]
        pk_gate = {}

        def gather_scale(e, first):
            p.wait("gpsimd", "sIX", vIX)
            p.wait("gpsimd", "sGF", gfull)
            if first:
                p.wait("vector", "sST", p.c["sST"])
            for j in range(5):
                col = 5 * e + j
                sga = "sGA" if j % 2 == 0 else "sGB"
                sgc = "sGC" if j % 2 == 0 else "sGD"
                if pk_gate.get(j % 2):
                    p.wait("gpsimd", "sVE", pk_gate[j % 2])
                vg1_ = p.op("gpsimd", lambda E, j=j, col=col: E.indirect_dma_start(
                    out=pk_raw[j % 2], out_offset=None,
                    in_=ln2d[:], in_offset=bass.IndirectOffsetOnAxis(
                        ap=idxs[:, col:col + 1], axis=0)), inc=sga, n=16)
                vg2_ = p.op("gpsimd", lambda E, j=j, col=col: E.indirect_dma_start(
                    out=mxb[:, 8 * (j % 2):8 * (j % 2) + 8], out_offset=None,
                    in_=gatd[:], in_offset=bass.IndirectOffsetOnAxis(
                        ap=idxs[:, col:col + 1], axis=0)), inc=sgc, n=16)
                p.wait("vector", sga, vg1_)
                p.wait("vector", sgc, vg2_)
                if not first:
                    p.wait("vector", "sPE", tr_mark)
                pk_gate[j % 2] = p.op("vector", lambda E, j=j, e=e: E.tensor_scalar(
                    out=pk_sc[j], in0=pk_raw[j % 2],
                    scalar1=mxb[:, 8 * (j % 2) + e:8 * (j % 2) + e + 1], scalar2=None,
                    op0=ALU.mult), inc="sVE")
            return p.c["sVE"]

        vsc_mark = gather_scale(0, True)

        tr_mark = 0
        pout_slot_mark = {0: 0, 1: 0}
        pout_marks = {}
        f15_prev = None
        for e in range(NE):
            sh_prev = ("sPE", p.c["sPE"])
            if e == 0:
                mark_hc0 = wcs.drain(16, dst_free=sh_ph2_done)
                mark_hc1 = wcs.drain(16, dst_free=sh_ph2_done)
                wod_cur = wcs.drain(16, dst_free=sh_ph2_done)
            else:
                mark_hc1 = wcs.drain(12, dst_free=("sPE", f15_prev))
                wod_cur = wcs.drain(16, dst_free=sh_prev)
            p.wait("tensor", "sVE", vsc_mark)
            for j in range(5):
                if j >= 1:
                    p.wait("tensor", "sVE", trev)
                for c in range(8):
                    vtp = p.op("tensor", lambda E, j=j, c=c: E.transpose(
                        out=bank_bf(0)[:, 128 * c:128 * (c + 1)],
                        in_=pk_sc[j][:, 128 * c:128 * (c + 1)], identity=identb[:]),
                        inc="sPE" if c == 7 else None)
                p.wait("vector", "sPE", vtp)
                trev = p.op("vector", lambda E, j=j: E.tensor_copy(
                    out=qh[:, 0:5120].rearrange("p (d c) -> p d c", c=640)[:, :, 128 * j:128 * (j + 1)],
                    in_=bank_bf(0)[:].rearrange("p (d c) -> p d c", c=128)), inc="sVE")
            tr_mark = p.c["sPE"]
            vrt = p.c["sVE"]
            if e + 1 < NE:
                load_expert_insw(e + 1)
                load_expert_wout(e + 1)

            p.wait("tensor", "sVE", vrt)
            last_h = {}
            f15_mark = None
            for f in range(16):
                base_c = 0 if f % 2 == 0 else 2048
                sw_c = 1024 if f % 2 == 0 else 3072
                if f == 0:
                    p.wait_marks("tensor", mark_hc0)
                if f == 8:
                    p.wait_marks("tensor", mark_hc1)
                if f >= 2:
                    p.wait("tensor", "sVE", last_h[f - 2])
                for mat, cbase in ((wbfA, base_c), (wbfB, sw_c)):
                    for d in range(8):
                        p.op("tensor", lambda E, f=f, d=d, mat=mat, cb=cbase: E.matmul(
                            out=PS[:, cb:cb + 512],
                            lhsT=mat[:, 2048 * d + 128 * f:2048 * d + 128 * f + 128],
                            rhs=rt_ap(d, 0, 512), start=(d == 0), stop=(d == 7)))
                    for d in range(8):
                        vmm = p.op("tensor", lambda E, f=f, d=d, mat=mat, cb=cbase: E.matmul(
                            out=PS[:, cb + 512:cb + 640],
                            lhsT=mat[:, 2048 * d + 128 * f:2048 * d + 128 * f + 128],
                            rhs=rt_ap(d, 512, 640), start=(d == 0), stop=(d == 7)),
                            inc="sPE" if (d == 7 and mat is wbfB) else None)
                if f == 15:
                    f15_mark = vmm
                p.wait("scalar", "sPE", vmm)
                vsi = p.op("scalar", lambda E, f=f, cb=base_c: E.activation(
                    out=m2sp[f % 2], in_=PS[:, cb:cb + 640], func=AF.Silu), inc="sAC")
                p.wait("vector", "sAC", vsi)
                last_h[f] = p.op("vector", lambda E, f=f, sc=sw_c: E.tensor_tensor(
                    out=h_ap(f, 0, 640), in0=m2sp[f % 2], in1=PS[:, sc:sc + 640],
                    op=ALU.mult), inc="sVE")
            vH = p.c["sVE"]
            if e + 1 < NE:
                vsc_mark = gather_scale(e + 1, False)

            f15_prev = f15_mark
            p.wait("tensor", "sVE", vH)
            p.wait_marks("tensor", wod_cur)
            last_po = {}
            for j in range(5):
                g0 = 4 if j % 2 == 0 else 6
                if j >= 2:
                    p.wait("tensor", "sVE", last_po[j - 2])
                p.wait("vector", "sPa" if j % 2 == 0 else "sPb", pout_slot_mark[j % 2])
                if e + 1 < NE and wcs.pending:
                    mark_hc0 = wcs.drain(4, dst_free=("sPE", f15_mark))
                for f in range(16):
                    for h2 in range(2):
                        vom = p.op("tensor", lambda E, f=f, j=j, h2=h2, g0=g0: E.matmul(
                            out=bank(g0 + h2),
                            lhsT=h_ap(f, 128 * j, 128 * (j + 1)),
                            rhs=wod[:, 1024 * f + 512 * h2:1024 * f + 512 * (h2 + 1)],
                            start=(f == 0), stop=(f == 15)),
                            inc="sPE" if (f == 15 and h2 == 1) else None)
                p.wait("vector", "sPE", vom)
                last_po[j] = p.op("vector", lambda E, j=j, g0=g0: E.tensor_copy(
                    out=pout[j % 2], in_=PS[:, 512 * g0:512 * g0 + 1024]), inc="sVE")
                p.wait("gpsimd", "sVE", last_po[j])
                sp_ = "sPa" if j % 2 == 0 else "sPb"
                p.dma("gpsimd", poutd[CAP * e + 128 * j:CAP * e + 128 * (j + 1), :],
                      pout[j % 2], sp_)
                pout_slot_mark[j % 2] = p.c[sp_]
        vPa, vPb = p.c["sPa"], p.c["sPb"]

        ob1 = [xh[:, 2048 + 1024 * s:2048 + 1024 * (s + 1)] for s in range(2)]
        ob2 = [xh[:, 4096 + 1024 * s:4096 + 1024 * (s + 1)] for s in range(2)]
        xb = [xh[:, 6144 + 1024 * s:6144 + 1024 * (s + 1)] for s in range(2)]
        wb_mark = {}
        st_mark = {}
        for g in range(16):
            sl = g % 2
            if g >= 2:
                p.wait("sync", "sPa" if g % 2 == 0 else "sPb", st_mark[g - 2])
                p.wait("gpsimd", "sVE", wb_mark[g - 2])
            sx = "sLXa" if g % 2 == 0 else "sLXb"
            p.dma("sync", xb[sl], xmid[128 * g:128 * (g + 1), :], sx)
            vxl = p.c[sx]
            if g == 0:
                p.wait("gpsimd", "sPa", vPa)
                p.wait("gpsimd", "sPb", vPb)
            sga = "sGA" if g % 2 == 0 else "sGB"
            sgc = "sGC" if g % 2 == 0 else "sGD"
            v1_ = p.op("gpsimd", lambda E, g=g, sl=sl: E.indirect_dma_start(
                out=ob1[sl], out_offset=None,
                in_=poutd[:], in_offset=bass.IndirectOffsetOnAxis(
                    ap=invs[:, 2 * g:2 * g + 1], axis=0)), inc=sga, n=16)
            v2_ = p.op("gpsimd", lambda E, g=g, sl=sl: E.indirect_dma_start(
                out=ob2[sl], out_offset=None,
                in_=poutd[:], in_offset=bass.IndirectOffsetOnAxis(
                    ap=invs[:, 2 * g + 1:2 * g + 2], axis=0)), inc=sgc, n=16)
            p.wait("vector", sga, v1_)
            p.wait("vector", sgc, v2_)
            p.wait("vector", sx, vxl)
            va1 = p.op("vector", lambda E, sl=sl: E.tensor_tensor(
                out=xb[sl], in0=xb[sl], in1=ob1[sl], op=ALU.add), inc="sVE")
            p.wait("vector", "sVE", va1)
            vwb = p.op("vector", lambda E, sl=sl: E.tensor_tensor(
                out=xb[sl], in0=xb[sl], in1=ob2[sl], op=ALU.add), inc="sVE")
            wb_mark[g] = vwb
            p.wait("gpsimd", "sVE", vwb)
            spq = "sPa" if g % 2 == 0 else "sPb"
            p.dma("gpsimd", y_out[128 * g:128 * (g + 1), :], xb[sl], spq)
            st_mark[g] = p.c[spq]
        p.wait("gpsimd", "sPa", p.c["sPa"])
        p.wait("gpsimd", "sPb", p.c["sPb"])

        with nc.Block() as block:
            @block.sync
            def _(E):
                for f in p.q["sync"]:
                    f(E)

            @block.tensor
            def _(E):
                for f in p.q["tensor"]:
                    f(E)

            @block.vector
            def _(E):
                for f in p.q["vector"]:
                    f(E)

            @block.scalar
            def _(E):
                for f in p.q["scalar"]:
                    f(E)

            @block.gpsimd
            def _(E):
                for f in p.q["gpsimd"]:
                    f(E)

    return nc


_CACHE = {}


def kernel(**inp):
    x = np.ascontiguousarray(np.asarray(inp["L0_attn_x"], dtype=np.float32))
    shared = {
        "wqT": np.ascontiguousarray(np.asarray(inp["L0_attn_Wq"], dtype=np.float32).T),
        "wkT": np.ascontiguousarray(np.asarray(inp["L0_attn_Wk"], dtype=np.float32).T),
        "wvT": np.ascontiguousarray(np.asarray(inp["L0_attn_Wv"], dtype=np.float32).T),
        "woT": np.ascontiguousarray(np.asarray(inp["L0_attn_Wo"], dtype=np.float32).T),
        "cbias": np.ascontiguousarray(np.stack([
            np.asarray(inp["L0_attn_bq"]), np.asarray(inp["L0_attn_bk"]),
            np.asarray(inp["L0_attn_bv"]), np.asarray(inp["L0_attn_bo"])]).astype(np.float32)),
        "router": np.ascontiguousarray(np.asarray(inp["L0_ffn_router_DE"], dtype=np.float32)),
        "w_in_e": np.ascontiguousarray(np.asarray(inp["L0_ffn_w_in_eDF"], dtype=np.float32)),
        "w_sw_e": np.ascontiguousarray(np.asarray(inp["L0_ffn_w_swiglu_eDF"], dtype=np.float32)),
        "w_out_e": np.ascontiguousarray(np.asarray(inp["L0_ffn_w_out_eFD"], dtype=np.float32)),
        "w_in_s": np.ascontiguousarray(np.asarray(inp["L0_ffn_w_in_shared_FD"], dtype=np.float32).T),
        "w_sw_s": np.ascontiguousarray(np.asarray(inp["L0_ffn_w_swiglu_FD"], dtype=np.float32).T),
        "w_out_s": np.ascontiguousarray(np.asarray(inp["L0_ffn_w_out_shared_DF"], dtype=np.float32).T),
        "c_id_bf": np.eye(128, dtype=ml_dtypes.bfloat16),
        "c_ones_bf": np.ones((1, 512), dtype=ml_dtypes.bfloat16),
        "c_tri": np.triu(np.ones((128, 128), dtype=np.float32), 1).T.astype(ml_dtypes.bfloat16),
        "c_onesf": np.ones((1, 128), dtype=np.float32),
        "c_tok": (np.arange(16, dtype=np.int32)[None, :] * 128
                  + np.arange(128, dtype=np.int32)[:, None]).astype(np.int32),
        "c_thr": ((np.arange(8, dtype=np.float32) + 1.0) * 640.0)[None, :],
        "c_base": (np.arange(8, dtype=np.float32) * 640.0)[None, :],
        "c_zidx": np.zeros((1, 8 * 640), dtype=np.int32),
    }
    selp = np.zeros((16, 1024), dtype=np.float32)
    for pr in range(8):
        for j in range(128):
            selp[2 * pr + (j // 64), 128 * pr + j] = 1.0
    shared["c_selp"] = selp

    if "nc" not in _CACHE:
        _CACHE["nc"] = build_program()
    nc = _CACHE["nc"]
    in_maps = [dict(x=np.ascontiguousarray(x[c]), **shared) for c in range(8)]
    res = run_bass_kernel_spmd(nc, in_maps, core_ids=list(range(8)))
    return np.stack([np.asarray(r["y"], dtype=np.float32) for r in res.results], axis=0)
